# version 01
# baseline (speedup 1.0000x reference)
import math
import numpy as np
import concourse.bass as bass
import concourse.mybir as mybir
from concourse.bass_utils import run_bass_kernel_spmd

F32 = mybir.dt.float32
BF16 = mybir.dt.bfloat16
AF = mybir.ActivationFunctionType
ALU = mybir.AluOpType
AX = mybir.AxisListType

NCORES = 8
B, T, DM = 2, 8192, 1024
TLOC = 2048
NSEG = 4
H, DH, VH = 8, 64, 128
CH = 128
NCH = TLOC // CH
GAMMA = 1.0 - 2.0 ** (-5.0 - np.arange(H, dtype=np.float64))

_BUILD_CACHE = {}
ENGS = ("sync", "gpsimd", "scalar", "vector", "tensor")


def _rot_tables():
    inv = 1.0 / 10000.0 ** (np.arange(0, DH, 2, dtype=np.float64) / DH)
    t = np.arange(T, dtype=np.float64)
    fr = t[:, None] * inv[None, :]
    emb = np.concatenate([fr, fr], 1)
    cos = np.cos(emb).T
    sin = np.sin(emb).T
    sinx = sin.copy()
    sinx[:DH // 2] *= -1.0
    cos2 = np.vstack([cos, cos]).astype(np.float32)
    sinx2 = np.vstack([sinx, sinx]).astype(np.float32)
    return cos2, sinx2


def _decay_consts():
    import ml_dtypes
    idx = np.arange(CH)
    rel = idx[None, :] - idx[:, None]
    maskT = np.where(rel >= 0, GAMMA[:, None, None] ** np.maximum(rel, 0)[None], 0.0) * (DH ** -0.5)
    wvecT = (GAMMA[:, None] ** (127 - idx)[None]).T
    gamvec = (DH ** -0.5) * GAMMA[:, None] ** (idx + 1)[None]
    gamvec_rep = np.broadcast_to(gamvec[:, None, :], (H, DH, CH)).copy()
    return (maskT.astype(ml_dtypes.bfloat16), wvecT.astype(np.float32),
            gamvec_rep.astype(ml_dtypes.bfloat16))


def _wcomb(seg):
    out = np.zeros((NSEG, 128, 4 * VH), dtype=np.float64)
    for j in range(seg):
        w = GAMMA ** (TLOC * (seg - 1 - j))
        for kc in range(4):
            out[j, 0:64, kc * VH:(kc + 1) * VH] = w[2 * kc]
            out[j, 64:128, kc * VH:(kc + 1) * VH] = w[2 * kc + 1]
    return out.astype(np.float32)


def _g128cols():
    g = np.empty((128, 4), dtype=np.float64)
    for kc in range(4):
        g[0:64, kc] = GAMMA[2 * kc] ** CH
        g[64:128, kc] = GAMMA[2 * kc + 1] ** CH
    return g.astype(np.float32)


def _gpcols():
    g = np.empty((128, NCH * 4), dtype=np.float64)
    for c in range(NCH):
        for kc in range(4):
            g[0:64, c * 4 + kc] = GAMMA[2 * kc] ** (CH * c)
            g[64:128, c * 4 + kc] = GAMMA[2 * kc + 1] ** (CH * c)
    return g.astype(np.float32)


class CntView:
    def __init__(self, sched):
        self.s = sched

    def __getitem__(self, qn):
        if qn in self.s.dmaq:
            return self.s.dcnt[qn]
        return len(self.s.marks.get(qn, []))


class Sched:

    SERIAL = ("vector", "scalar", "gpsimd")

    def __init__(self, name):
        self.nc = bass.Bass("TRN2", target_bir_lowering=False)
        self._sem = {}
        self.marks = {}
        self.dmaq = set()
        self.dcnt = {}
        self.ecnt = {e: 0 for e in ENGS}
        self.q = {e: [] for e in ENGS}
        self.name = name
        self.cnt = CntView(self)

    def esem(self, eng):
        key = "E_" + eng
        if key not in self._sem:
            self._sem[key] = self.nc.alloc_semaphore(key)
        return self._sem[key]

    def qsem(self, qn):
        if qn not in self._sem:
            self._sem[qn] = self.nc.alloc_semaphore(qn)
            self.dmaq.add(qn)
            self.dcnt[qn] = 0
        return self._sem[qn]

    def dma(self, eng, qn, out, in_):
        sem = self.qsem(qn)
        self.dcnt[qn] += 16
        self.q[eng].append(lambda E, o=out, i=in_, s=sem: E.dma_start(out=o, in_=i).then_inc(s, 16))
        return self.dcnt[qn]

    def op(self, eng, fn, qn=None, n=1):
        sem = self.esem(eng)
        self.ecnt[eng] += 1
        ec = self.ecnt[eng]
        selfwait = (ec - 1) if (eng in self.SERIAL and ec > 1) else 0

        def emit(E, f=fn, s=sem, w=selfwait):
            if w:
                E.wait_ge(s, w)
            f(E).then_inc(s, 1)
        self.q[eng].append(emit)
        if qn is not None:
            self.marks.setdefault(qn, []).append((eng, ec))
        return len(self.marks.get(qn, [])) if qn else None

    def wait(self, eng, qn, val=None):
        if qn in self.dmaq:
            v = self.dcnt[qn] if val is None else val
            sem = self._sem[qn]
            self.q[eng].append(lambda E, s=sem, v=v: E.wait_ge(s, v))
            return
        ms = self.marks.get(qn, [])
        idx = len(ms) if val is None else val
        if idx <= 0:
            return
        assert ms and idx <= len(ms), f"wait on {qn} ordinal {idx} but only {len(ms)} marks"
        src_eng, ec = ms[idx - 1]
        sem = self.esem(src_eng)
        self.q[eng].append(lambda E, s=sem, v=ec: E.wait_ge(s, v))

    def build(self):
        with self.nc.Block() as blk:
            for name in ENGS:
                if not self.q[name]:
                    continue
                fns = self.q[name]

                def body(E, fns=fns):
                    for f in fns:
                        f(E)
                getattr(blk, name)(body)
        return self.nc


class TP:

    def __init__(self, s, ps_T, ident):
        self.s, self.ps_T, self.ident = s, ps_T, ident
        self.last = None

    def run(self, src_ap, consumer_eng, consume_fn, cons_qn="tpc"):
        s = self.s
        if self.last is not None:
            s.wait("tensor", self.last[0], self.last[1])
        pdim = src_ap.shape[0]
        base = src_ap.base_partition()
        t_tr = s.op("tensor",
                    lambda E, src=src_ap, pdim=pdim, base=base: E.transpose(
                        self.ps_T[:, 0:pdim], src, self.ident[base:base + pdim, base:base + pdim]),
                    "pe_T")
        s.wait(consumer_eng, "pe_T", t_tr)
        t_c = s.op(consumer_eng, consume_fn, cons_qn)
        self.last = (cons_qn, t_c)
        return t_tr, t_c


def emit_rsqrt(s, out_ap, w_ap, tmp_ap, tmp2_ap, iters=18, qn=None):
    s.op("vector", lambda E: E.reciprocal(out_ap, w_ap))
    s.op("vector", lambda E: E.tensor_scalar(out_ap, out_ap, 1.0, None, ALU.min))
    for i in range(iters):
        s.op("vector", lambda E: E.tensor_tensor(tmp_ap, out_ap, out_ap, ALU.mult))
        s.op("vector", lambda E: E.tensor_tensor(tmp2_ap, tmp_ap, w_ap, ALU.mult))
        s.op("vector", lambda E: E.tensor_scalar(tmp2_ap, tmp2_ap, -0.5, 1.5, ALU.mult, ALU.add))
        s.op("vector", lambda E: E.tensor_tensor(out_ap, out_ap, tmp2_ap, ALU.mult),
             qn if i == iters - 1 else None)


def emit_stats_finish(s, tag, stat, sw, mb, rb, rows_scr, rowsb_scr, dm, tl=TLOC,
                      eps=1e-5, wait_q=None, wait_v=None):
    f = tl // 128
    if wait_q:
        s.wait("vector", wait_q, wait_v)
    s.op("vector", lambda E: E.tensor_scalar(sw["stm"][:, 0:f], stat[:, 0:f], 1.0 / dm, None, ALU.mult))
    s.op("vector", lambda E: E.tensor_scalar(sw["stv"][:, 0:f], stat[:, f:2 * f], 1.0 / dm, None, ALU.mult))
    s.op("vector", lambda E: E.tensor_tensor(sw["stt1"][:, 0:f], sw["stm"][:, 0:f], sw["stm"][:, 0:f], ALU.mult))
    s.op("vector", lambda E: E.tensor_tensor(sw["stv"][:, 0:f], sw["stv"][:, 0:f], sw["stt1"][:, 0:f], ALU.subtract))
    s.op("vector", lambda E: E.tensor_scalar(sw["stv"][:, 0:f], sw["stv"][:, 0:f], eps, None, ALU.add))
    emit_rsqrt(s, sw["str_"][:, 0:f], sw["stv"][:, 0:f], sw["stt1"][:, 0:f], sw["stt2"][:, 0:f])
    s.op("vector", lambda E: E.tensor_copy(sw["mrow"][:, 0:f], sw["stm"][:, 0:f]))
    s.op("vector", lambda E: E.tensor_copy(sw["rrow"][:, 0:f], sw["str_"][:, 0:f]), "dve_st_" + tag)
    s.wait("gpsimd", "dve_st_" + tag)
    s.dma("gpsimd", "gq", rowsb_scr[0:1, 0:tl].rearrange("a (p x) -> (a p) x", p=128), sw["mrow"][:, 0:f])
    s.dma("gpsimd", "gq", rowsb_scr[0:1, tl:2 * tl].rearrange("a (p x) -> (a p) x", p=128), sw["rrow"][:, 0:f])
    s.wait("gpsimd", "gq")
    s.dma("gpsimd", "gq", mb[:, 0:tl] if hasattr(mb, "ap") and not isinstance(mb, type(rowsb_scr)) else mb[:, 0:tl],
          rowsb_scr[0:1, 0:tl].broadcast_to((128, tl)))
    return s.dma("gpsimd", "gq", rb[:, 0:tl], rowsb_scr[0:1, tl:2 * tl].broadcast_to((128, tl)))


def emit_stats_rows_to_sbuf(s, stat, rows_scr, tl=TLOC, wait_q=None, wait_v=None):
    f = tl // 128
    if wait_q:
        s.wait("gpsimd", wait_q, wait_v)
    s.dma("gpsimd", "gq", stat[:, 0:f], rows_scr[0:1, 0:tl].rearrange("a (p x) -> (a p) x", p=128))
    return s.dma("gpsimd", "gq", stat[:, f:2 * f],
                 rows_scr[0:1, tl:2 * tl].rearrange("a (p x) -> (a p) x", p=128))


def emit_ln_stats(s, tag, xin, nch, onescol, xsqAB, ps_sx, ps_sq, rows_scr, srow2, tl=TLOC):
    nb = tl // 512
    act_q, pe_q, st_q = "act_sq" + tag, "pe_sq" + tag, "pe_st" + tag
    for tt in range(nb):
        if tt >= 1:
            s.wait("tensor", "gq_st" + tag, tt * 32)
        for c in range(nch):
            k = tt * nch + c
            xs = xsqAB[k % 2]
            if k >= 2:
                s.wait("scalar", pe_q, k - 1)
            s.op("scalar",
                 lambda E, xs=xs, c=c, tt=tt: E.activation(
                     xs[:], xin[:, c, tt * 512:(tt + 1) * 512], AF.Square),
                 act_q)
            s.op("tensor",
                 lambda E, c=c, tt=tt: E.matmul(ps_sx[:], onescol[:],
                                                xin[:, c, tt * 512:(tt + 1) * 512],
                                                start=(c == 0), stop=(c == nch - 1)),
                 st_q if c == nch - 1 else None)
            s.wait("tensor", act_q, k + 1)
            s.op("tensor",
                 lambda E, xs=xs, c=c: E.matmul(ps_sq[:], onescol[:], xs[:],
                                                start=(c == 0), stop=(c == nch - 1)),
                 pe_q)
        s.wait("vector", st_q, s.cnt[st_q])
        s.wait("vector", pe_q, s.cnt[pe_q])
        if tt >= 1:
            s.wait("vector", "gq_st" + tag, tt * 32)
        s.op("vector", lambda E: E.tensor_copy(srow2[0:1, 0:512], ps_sx[:]))
        s.op("vector", lambda E: E.tensor_copy(srow2[0:1, 512:1024], ps_sq[:]), "dve_sr" + tag)
        s.wait("gpsimd", "dve_sr" + tag, s.cnt["dve_sr" + tag])
        s.dma("gpsimd", "gq_st" + tag, rows_scr[0:1, tt * 512:(tt + 1) * 512], srow2[0:1, 0:512])
        s.dma("gpsimd", "gq_st" + tag, rows_scr[0:1, tl + tt * 512:tl + (tt + 1) * 512], srow2[0:1, 512:1024])
    return s.cnt["gq_st" + tag]


def build_l0a1():
    s = Sched("l0a1")
    nc = s.nc
    xT = nc.dram_tensor("xT", [DM, TLOC], F32, kind="ExternalInput").ap()
    wkvT = nc.dram_tensor("wkvT", [DM, 1536], F32, kind="ExternalInput").ap()
    cos2d = nc.dram_tensor("cos2", [128, TLOC], F32, kind="ExternalInput").ap()
    sinx2d = nc.dram_tensor("sinx2", [128, TLOC], F32, kind="ExternalInput").ap()
    wvecTd = nc.dram_tensor("wvecT", [128, H], F32, kind="ExternalInput").ap()
    g128d = nc.dram_tensor("g128cols", [128, 4], F32, kind="ExternalInput").ap()
    identd = nc.dram_tensor("identd", [128, 128], BF16, kind="ExternalInput").ap()
    onesd = nc.dram_tensor("ones", [128, 1], F32, kind="ExternalInput").ap()

    xn_out = nc.dram_tensor("xn_out", [128, 8 * TLOC], BF16, kind="ExternalOutput").ap()
    krot_out = nc.dram_tensor("krot_out", [128, 4 * TLOC], BF16, kind="ExternalOutput").ap()
    vT_out = nc.dram_tensor("vT_out", [128, NCH * H * 128], BF16, kind="ExternalOutput").ap()
    st_out = nc.dram_tensor("st_out", [128, NCH * 4 * 128], BF16, kind="ExternalOutput").ap()
    summ_out = nc.dram_tensor("summ_out", [128, 4 * 128], F32, kind="ExternalOutput").ap()
    rows_scr = nc.dram_tensor("rows_scr", [1, 2 * TLOC], F32, kind="Internal").ap()
    rowsb_scr = nc.dram_tensor("rowsb_scr", [1, 2 * TLOC], BF16, kind="Internal").ap()

    A = nc.alloc_sbuf_tensor
    xin = A("xin", [128, 8, TLOC], F32)
    xn = A("xn", [128, 8, TLOC], BF16)
    cosS = A("cosS", [128, 512], F32)
    sinS = A("sinS", [128, 512], F32)
    xsqA = A("xsqA", [128, 512], F32)
    xsqB = A("xsqB", [128, 512], F32)
    wtmpA = A("wtmpA", [128, 512], F32)
    wtmpB = A("wtmpB", [128, 512], F32)
    wkvb = A("wkvb", [128, 8, 1536], BF16)
    krot = A("krot", [128, 4, TLOC], BF16)
    vT = A("vT", [128, NCH * H * 128], BF16)
    kfp = A("kfp", [128, 512], F32)
    vtmpA = A("vtmpA", [128, 512], BF16)
    vtmpB = A("vtmpB", [128, 512], BF16)
    ident = A("ident", [128, 128], BF16)
    onescol = A("onescol", [128, 1], F32)
    wvcols = A("wvcols", [128, H], F32)
    g128c_sb = A("g128c_sb", [128, 4], F32)
    kTw = A("kTw", [128, 64], BF16)
    Lcur = A("Lcur", [128, 4, 128], F32)
    LoutA = A("LoutA", [128, 128], BF16)
    LoutB = A("LoutB", [128, 128], BF16)
    stat = A("stat", [128, 32], F32)
    sw = {k: A(k, [128, 16], F32) for k in ("stm", "stv", "str_", "stt1", "stt2")}
    sw["mrow"] = A("mrow", [128, 16], BF16)
    sw["rrow"] = A("rrow", [128, 16], BF16)
    mb = A("mb", [128, TLOC], BF16)
    rb = A("rb", [128, TLOC], BF16)
    nrmA = A("nrmA", [128, 512], F32)
    nrmB = A("nrmB", [128, 512], F32)
    ktmp, kswap = nrmA, nrmB
    srow2 = A("srow2", [1, 1024], F32)

    P = nc.alloc_psum_tensor
    ps_proj = P("ps_proj", [128, 512], F32)
    ps_sx = P("ps_sx", [1, 512], F32)
    ps_sq = P("ps_sq", [1, 512], F32)
    ps_T = P("ps_T", [128, 128], BF16)
    ps_st = P("ps_st", [128, 128], F32)
    tp = TP(s, ps_T, ident)

    s.dma("sync", "sy_x", xin[:], xT.rearrange("(c p) t -> p c t", p=128))
    s.dma("sync", "sy_c", ident[:], identd)
    s.dma("sync", "sy_c", onescol[:], onesd)
    s.dma("sync", "sy_c", wvcols[:], wvecTd)
    t_consts = s.dma("sync", "sy_c", g128c_sb[:], g128d)
    wkvT_r = wkvT.rearrange("(c p) (u o) -> c u p o", p=128, o=512)
    for cu in range(24):
        wt = (wtmpA, wtmpB)[cu % 2]
        wq = "sy_w" + str(cu % 2)
        if cu >= 2:
            s.wait("sync", "dve_w", cu - 1)
        s.dma("sync", wq, wt[:], wkvT_r[cu // 3, cu % 3])
        s.wait("vector", wq, s.cnt[wq])
        c, u = cu // 3, cu % 3
        s.op("vector",
             lambda E, c=c, u=u, wt=wt: E.tensor_copy(wkvb[:, c, u * 512:(u + 1) * 512], wt[:]),
             "dve_w")

    s.wait("scalar", "sy_x", 16)
    s.wait("tensor", "sy_x", 16)
    s.wait("tensor", "sy_c", t_consts)
    emit_ln_stats(s, "ln", xin, 8, onescol, (xsqA, xsqB), ps_sx, ps_sq, rows_scr, srow2)
    t_rows = emit_stats_rows_to_sbuf(s, stat, rows_scr, wait_q="gq_stln", wait_v=s.cnt["gq_stln"])
    s.wait("vector", "gq", t_rows)
    t_bc = emit_stats_finish(s, "ln", stat, sw, mb, rb, rows_scr, rowsb_scr, DM)
    s.wait("vector", "gq", t_bc)
    for c in range(8):
        for tt in range(4):
            sl = slice(tt * 512, (tt + 1) * 512)
            nb_ = (nrmA, nrmB)[tt % 2]
            s.op("vector", lambda E, c=c, sl=sl, nb_=nb_: E.tensor_tensor(nb_[:], xin[:, c, sl], mb[:, sl], ALU.subtract))
            s.op("vector", lambda E, c=c, sl=sl, nb_=nb_: E.tensor_tensor(xn[:, c, sl], nb_[:], rb[:, sl], ALU.mult),
                 "dve_xn" if (c == 7 and tt == 3) else None)
    t_xn = s.cnt["dve_xn"]
    s.wait("gpsimd", "dve_xn", t_xn)
    s.dma("gpsimd", "gq", xn_out, xn[:].rearrange("p c t -> p (c t)"))

    s.wait("tensor", "dve_xn", t_xn)
    s.wait("tensor", "dve_w", 24)
    for oc in range(12):
        for tt in range(4):
            idx = oc * 4 + tt
            if idx >= 1:
                s.wait("tensor", "evac", idx)
            for c in range(8):
                s.op("tensor",
                     lambda E, oc=oc, c=c, tt=tt: E.matmul(
                         ps_proj[:], wkvb[:, c, oc * 128:(oc + 1) * 128],
                         xn[:, c, tt * 512:(tt + 1) * 512],
                         start=(c == 0), stop=(c == 7)),
                     "pe_proj" if c == 7 else None)
            tpj = s.cnt["pe_proj"]
            sl = slice(tt * 512, (tt + 1) * 512)
            if oc < 4:
                s.wait("vector", "pe_proj", tpj)
                t_kfp = s.op("vector", lambda E: E.tensor_copy(kfp[:], ps_proj[:]), "evac")
                s.wait("gpsimd", "evac", t_kfp)
                if idx >= 1:
                    s.wait("gpsimd", "dve_krot", idx)
                s.dma("gpsimd", "gsw", cosS[:], cos2d[:, sl])
                s.dma("gpsimd", "gsw", sinS[:], sinx2d[:, sl])
                for half in range(4):
                    src = kfp[[32, 0, 96, 64][half]:[64, 32, 128, 96][half], :]
                    dst = kswap[[0, 32, 64, 96][half]:[32, 64, 96, 128][half], :]
                    t_sw2 = s.dma("gpsimd", "gsw", dst, src)
                s.wait("vector", "gsw", t_sw2)
                s.op("vector", lambda E: E.tensor_tensor(ktmp[:], kfp[:], cosS[:], ALU.mult))
                s.op("vector", lambda E: E.tensor_tensor(kswap[:], kswap[:], sinS[:], ALU.mult))
                s.op("vector", lambda E, oc=oc, sl=sl: E.tensor_tensor(krot[:, oc, sl], ktmp[:], kswap[:], ALU.add),
                     "dve_krot")
            else:
                h = oc - 4
                vt = (vtmpA, vtmpB)[tt % 2]
                s.wait("vector", "pe_proj", tpj)
                jv = idx - 16
                if jv >= 2:
                    s.wait("vector", "dve_T", (jv - 1) * 4)
                t_vt = s.op("vector", lambda E, vt=vt: E.tensor_copy(vt[:], ps_proj[:]), "evac")
                s.wait("tensor", "evac", t_vt)
                for cc in range(4):
                    cglob = tt * 4 + cc
                    i = cglob * H + h
                    tp.run(vt[:, cc * 128:(cc + 1) * 128], "vector",
                           lambda E, i=i: E.tensor_copy(vT[:, i * 128:(i + 1) * 128], ps_T[:]), "dve_T")
    t_vT_all = s.cnt["dve_T"]
    t_krot_all = s.cnt["dve_krot"]
    s.wait("gpsimd", "dve_krot", t_krot_all)
    s.dma("gpsimd", "gq", krot_out, krot[:].rearrange("p c t -> p (c t)"))
    s.wait("gpsimd", "dve_T", t_vT_all)
    s.dma("gpsimd", "gq", vT_out, vT[:])

    s.op("vector", lambda E: E.memset(Lcur[:], 0.0), "dve_L")
    s.wait("tensor", "dve_T", t_vT_all)
    s.wait("tensor", "dve_krot", t_krot_all)
    s.wait("tensor", "dve_L", s.cnt["dve_L"])
    for c in range(NCH):
        for kc in range(4):
            ik = c * 4 + kc
            Lo = (LoutA, LoutB)[ik % 2]
            gq_st_i = "gq_stt" + str(ik % 2)
            s.wait("vector", "dve_L", s.cnt["dve_L"])
            if ik >= 2:
                s.wait("vector", gq_st_i, (ik // 2) * 16)
            s.op("vector", lambda E, Lo=Lo, kc=kc: E.tensor_copy(Lo[:], Lcur[:, kc, :]), "dve_Lo")
            s.wait("gpsimd", "dve_Lo", s.cnt["dve_Lo"])
            s.dma("gpsimd", gq_st_i, st_out[:, ik * 128:(ik + 1) * 128], Lo[:])
            for hh in range(2):
                h = 2 * kc + hh
                i = c * H + h
                pbase = hh * 64
                if ik * 2 + hh >= 1:
                    s.wait("tensor", "pe_stt", ik * 2 + hh)
                tp.run(krot[pbase:pbase + 64, kc, c * 128:(c + 1) * 128], "vector",
                       lambda E, h=h: E.tensor_scalar(kTw[:, 0:64], ps_T[:, 0:64],
                                                      wvcols[:, h:h + 1], None, ALU.mult),
                       "dve_kTw")
                s.wait("tensor", "dve_kTw", s.cnt["dve_kTw"])
                s.op("tensor",
                     lambda E, i=i, pbase=pbase: E.matmul(
                         ps_st[pbase:pbase + 64, :], kTw[:, 0:64],
                         vT[:, i * 128:(i + 1) * 128],
                         start=True, stop=True),
                     "pe_stt")
            s.wait("vector", "pe_stt", s.cnt["pe_stt"])
            s.op("vector",
                 lambda E, kc=kc: E.scalar_tensor_tensor(
                     Lcur[:, kc, :], Lcur[:, kc, :],
                     g128c_sb[:, kc:kc + 1], ps_st[:], ALU.mult, ALU.add),
                 "dve_L")
    s.wait("gpsimd", "dve_L", s.cnt["dve_L"])
    s.dma("gpsimd", "gq", summ_out, Lcur[:].rearrange("p k t -> p (k t)"))
    s.wait("gpsimd", "gq", s.cnt["gq"])
    s.wait("gpsimd", "gq_stt0", s.cnt["gq_stt0"])
    s.wait("gpsimd", "gq_stt1", s.cnt["gq_stt1"])
    s.wait("gpsimd", "gq_stln", s.cnt["gq_stln"])
    return s.build()


def build_l0a2():
    s = Sched("l0a2")
    nc = s.nc
    xT = nc.dram_tensor("xT", [DM, TLOC], F32, kind="ExternalInput").ap()
    xn_in = nc.dram_tensor("xn_in", [128, 8 * TLOC], BF16, kind="ExternalInput").ap()
    krot_in = nc.dram_tensor("krot_in", [128, 4 * TLOC], BF16, kind="ExternalInput").ap()
    vT_in = nc.dram_tensor("vT_in", [128, NCH * H * 128], BF16, kind="ExternalInput").ap()
    st_in = nc.dram_tensor("st_in", [128, NCH * 4 * 128], BF16, kind="ExternalInput").ap()
    summs = nc.dram_tensor("summs", [NSEG * 128, 4 * 128], F32, kind="ExternalInput").ap()
    gpd = nc.dram_tensor("gpcols", [128, NCH * 4], F32, kind="ExternalInput").ap()
    wqgT = nc.dram_tensor("wqgT", [DM, 1536], F32, kind="ExternalInput").ap()
    woT = nc.dram_tensor("woT", [DM, DM], F32, kind="ExternalInput").ap()
    cos2d = nc.dram_tensor("cos2", [128, TLOC], F32, kind="ExternalInput").ap()
    sinx2d = nc.dram_tensor("sinx2", [128, TLOC], F32, kind="ExternalInput").ap()
    maskTd = nc.dram_tensor("maskT", [H * 128, 128], BF16, kind="ExternalInput").ap()
    gamvd = nc.dram_tensor("gamv", [H * DH, 128], BF16, kind="ExternalInput").ap()
    wcombd = nc.dram_tensor("wcomb", [NSEG * 128, 4 * 128], F32, kind="ExternalInput").ap()
    gnwd = nc.dram_tensor("gnw", [1, DM], F32, kind="ExternalInput").ap()
    gnbd = nc.dram_tensor("gnb", [1, DM], F32, kind="ExternalInput").ap()
    identd = nc.dram_tensor("identd", [128, 128], BF16, kind="ExternalInput").ap()
    onesd = nc.dram_tensor("ones", [128, 1], F32, kind="ExternalInput").ap()

    x1_out = nc.dram_tensor("x1_out", [DM, TLOC], F32, kind="ExternalOutput").ap()
    rows_scr = nc.dram_tensor("rows_scr", [1, 2 * TLOC], F32, kind="Internal").ap()
    rowsb_scr = nc.dram_tensor("rowsb_scr", [1, 2 * TLOC], BF16, kind="Internal").ap()

    A = nc.alloc_sbuf_tensor
    xnoT = A("xnoT", [128, 8 * TLOC], BF16)
    kqw = A("kqw", [128, 8, TLOC], BF16)
    krot = kqw[:, 0:4, :]
    qrot = kqw[:, 4:8, :]
    woutb = kqw[:, :, :]
    sgT = A("sgT", [128, NCH * H * 128], BF16)
    wqgb = A("wqgb", [128, 8, 1536], BF16)
    S0 = A("S0", [128, 4 * 128], F32)
    sumA = A("sumA", [128, 4 * 128], F32)
    wcbA = A("wcbA", [128, 4 * 128], F32)
    gpcols = A("gpcols_sb", [128, NCH * 4], F32)
    sctA = A("sctA", [128, 128], BF16)
    sctB = A("sctB", [128, 128], BF16)
    stA = A("stA", [128, 128], BF16)
    stB = A("stB", [128, 128], BF16)
    vTA = A("vTA", [128, 128], BF16)
    vTB = A("vTB", [128, 128], BF16)
    qd2A = A("qd2A", [128, 128], BF16)
    qd2B = A("qd2B", [128, 128], BF16)
    gamv_sb = A("gamv_sb", [128, 4, 128], BF16)
    maskT_sb = A("maskT_sb", [128, H, 128], BF16)
    gnwb = A("gnwb", [128, H, 128], BF16)
    gnbb = A("gnbb", [128, H, 128], BF16)
    cosS = A("cosS", [128, 512], F32)
    sinS = A("sinS", [128, 512], F32)
    wtmpA = A("wtmpA", [128, 512], F32)
    wtmpB = A("wtmpB", [128, 512], F32)
    kfp = A("kfp", [128, 512], F32)
    kswap = A("kswap", [128, 512], F32)
    ktmp = A("ktmp", [128, 512], F32)
    gtmpA = A("gtmpA", [128, 512], BF16)
    gtmpB = A("gtmpB", [128, 512], BF16)
    ident = A("ident", [128, 128], BF16)
    onescol = A("onescol", [128, 1], F32)
    onescolb = A("onescolb", [128, 1], BF16)
    mskA = A("mskA", [128, 128], BF16)
    mskB = A("mskB", [128, 128], BF16)
    obA = A("obA", [128, 128], BF16)
    obB = A("obB", [128, 128], BF16)
    osq_junk = A("osq_junk", [128, 128], BF16)
    gso = A("gso", [128, NCH * H], F32)
    gsq = A("gsq", [128, NCH * H], F32)
    gm = A("gm", [128, NCH * H], F32)
    gv = A("gv", [128, NCH * H], F32)
    gr = A("gr", [128, NCH * H], F32)
    gt1 = A("gt1", [128, NCH * H], F32)
    gt2 = A("gt2", [128, NCH * H], F32)
    onB = A("onB", [128, 128], F32)
    onC = A("onC", [128, 128], BF16)
    wsqA = A("wsqA", [128, 512], F32)
    wsqB = A("wsqB", [128, 512], F32)
    xresA = A("xresA", [128, TLOC], F32)
    xresB = A("xresB", [128, TLOC], F32)
    stat = A("stat", [128, 32], F32)
    sw = {k: A(k, [128, 16], F32) for k in ("stm", "stv", "str_", "stt1", "stt2")}
    sw["mrow"] = A("mrow", [128, 16], BF16)
    sw["rrow"] = A("rrow", [128, 16], BF16)
    mb = A("mb", [128, TLOC], BF16)
    rb = A("rb", [128, TLOC], BF16)
    xout = A("xout", [128, TLOC], F32)
    srow2 = A("srow2", [1, 1024], F32)

    P = nc.alloc_psum_tensor
    ps_proj = P("ps_proj", [128, 512], F32)
    ps_sc = P("ps_sc", [128, 128], F32)
    ps_o = P("ps_o", [128, 128], F32)
    ps_T = P("ps_T", [128, 128], BF16)
    ps_sx = P("ps_sx", [1, 512], F32)
    ps_sq = P("ps_sq", [1, 512], F32)
    tp = TP(s, ps_T, ident)
    xn = xnoT[:].rearrange("p (c t) -> p c t", c=8)

    s.dma("sync", "sy_b", xnoT[:], xn_in)
    s.dma("sync", "sy_b", krot.rearrange("p c t -> p (c t)"), krot_in)
    s.dma("sync", "sy_b", gamv_sb[:], gamvd.rearrange("(k p) t -> p k t", p=128))
    s.dma("sync", "sy_b", gpcols[:], gpd)
    s.dma("sync", "sy_b", maskT_sb[:], maskTd.rearrange("(h p) t -> p h t", p=128))
    s.dma("sync", "sy_b", ident[:], identd)
    s.dma("sync", "sy_b", onescol[:], onesd)
    s.dma("gpsimd", "gq_b", onescolb[:], onesd)
    for h in range(H):
        s.dma("gpsimd", "gq_b", gnwb[:, h, :], gnwd[0:1, h * 128:(h + 1) * 128].broadcast_to((128, 128)))
        s.dma("gpsimd", "gq_b", gnbb[:, h, :], gnbd[0:1, h * 128:(h + 1) * 128].broadcast_to((128, 128)))
    t_base = s.cnt["sy_b"]
    s.op("vector", lambda E: E.memset(S0[:], 0.0))
    for j in range(NSEG):
        s.dma("gpsimd", "gq", sumA[:], summs.rearrange("(j p) f -> j p f", p=128)[j])
        t_sj = s.dma("gpsimd", "gq", wcbA[:], wcombd.rearrange("(j p) f -> j p f", p=128)[j])
        s.wait("vector", "gq", t_sj)
        s.op("vector", lambda E: E.tensor_tensor(wcbA[:], sumA[:], wcbA[:], ALU.mult))
        t_S0 = s.op("vector", lambda E: E.tensor_tensor(S0[:], S0[:], wcbA[:], ALU.add), "dve_S0")
        if j < NSEG - 1:
            s.wait("gpsimd", "dve_S0", t_S0)
    wqgT_r = wqgT.rearrange("(c p) (u o) -> c u p o", p=128, o=512)
    for cu in range(24):
        wt = (wtmpA, wtmpB)[cu % 2]
        wq = "sy_w" + str(cu % 2)
        if cu >= 2:
            s.wait("sync", "dve_w", cu - 1)
        s.dma("sync", wq, wt[:], wqgT_r[cu // 3, cu % 3])
        s.wait("vector", wq, s.cnt[wq])
        c, u = cu // 3, cu % 3
        s.op("vector",
             lambda E, c=c, u=u, wt=wt: E.tensor_copy(wqgb[:, c, u * 512:(u + 1) * 512], wt[:]),
             "dve_w")

    s.wait("tensor", "sy_b", t_base)
    s.wait("tensor", "dve_w", 24)
    for oc in range(12):
        for tt in range(4):
            idx = oc * 4 + tt
            if idx >= 1:
                s.wait("tensor", "evac", idx)
            for c in range(8):
                s.op("tensor",
                     lambda E, oc=oc, c=c, tt=tt: E.matmul(
                         ps_proj[:], wqgb[:, c, oc * 128:(oc + 1) * 128],
                         xn[:, c, tt * 512:(tt + 1) * 512],
                         start=(c == 0), stop=(c == 7)),
                     "pe_proj" if c == 7 else None)
            tpj = s.cnt["pe_proj"]
            sl = slice(tt * 512, (tt + 1) * 512)
            if oc < 4:
                s.wait("vector", "pe_proj", tpj)
                t_kfp = s.op("vector", lambda E: E.tensor_copy(kfp[:], ps_proj[:]), "evac")
                s.wait("gpsimd", "evac", t_kfp)
                if idx >= 1:
                    s.wait("gpsimd", "dve_qrot", idx)
                s.dma("gpsimd", "gsw", cosS[:], cos2d[:, sl])
                s.dma("gpsimd", "gsw", sinS[:], sinx2d[:, sl])
                for half in range(4):
                    src = kfp[[32, 0, 96, 64][half]:[64, 32, 128, 96][half], :]
                    dst = kswap[[0, 32, 64, 96][half]:[32, 64, 96, 128][half], :]
                    t_sw2 = s.dma("gpsimd", "gsw", dst, src)
                s.wait("vector", "gsw", t_sw2)
                s.op("vector", lambda E: E.tensor_tensor(ktmp[:], kfp[:], cosS[:], ALU.mult))
                s.op("vector", lambda E: E.tensor_tensor(kswap[:], kswap[:], sinS[:], ALU.mult))
                s.op("vector", lambda E, oc=oc, sl=sl: E.tensor_tensor(qrot[:, oc, sl], ktmp[:], kswap[:], ALU.add),
                     "dve_qrot")
            else:
                h = oc - 4
                gt = (gtmpA, gtmpB)[tt % 2]
                s.wait("vector", "pe_proj", tpj)
                jg = idx - 16
                if jg >= 2:
                    s.wait("vector", "act_sg", (jg - 1) * 4)
                t_gt = s.op("vector", lambda E, gt=gt: E.tensor_copy(gt[:], ps_proj[:]), "evac")
                s.wait("tensor", "evac", t_gt)
                for cc in range(4):
                    cglob = tt * 4 + cc
                    i = cglob * H + h
                    tp.run(gt[:, cc * 128:(cc + 1) * 128], "scalar",
                           lambda E, i=i: E.activation(sgT[:, i * 128:(i + 1) * 128], ps_T[:], AF.Silu),
                           "act_sg")
    t_sg = s.cnt["act_sg"]
    t_qrot = s.cnt["dve_qrot"]

    woT_r = woT.rearrange("(c p) (u o) -> c u p o", p=128, o=512)
    s.wait("sync", "pe_proj", s.cnt["pe_proj"])
    for cu in range(16):
        wt = (wtmpA, wtmpB)[cu % 2]
        wq = "sy_w" + str(cu % 2)
        s.wait("sync", "dve_w", 23 + cu)
        s.dma("sync", wq, wt[:], woT_r[cu // 2, cu % 2])
        s.wait("vector", wq, s.cnt[wq])
        c, u = cu // 2, cu % 2
        s.op("vector",
             lambda E, c=c, u=u, wt=wt: E.tensor_copy(wqgb[:, c, u * 512:(u + 1) * 512], wt[:]),
             "dve_w")
    t_wo = s.cnt["dve_w"]

    s.wait("tensor", "dve_qrot", t_qrot)
    s.wait("tensor", "dve_S0", s.cnt["dve_S0"])
    s.wait("scalar", "act_sg", t_sg)
    s.wait("vector", "gq_b", s.cnt["gq_b"])
    s.wait("tensor", "gq_b", s.cnt["gq_b"])
    tr_marks = {}
    for c in range(NCH):
        for kc in range(4):
            ik = c * 4 + kc
            csl = slice(c * 128, (c + 1) * 128)
            st = (stA, stB)[ik % 2]
            sct = (sctA, sctB)[ik % 2]
            qd2 = (qd2A, qd2B)[ik % 2]
            gin_i = "gin" + str(ik % 2)
            if ik >= 2:
                s.wait("gpsimd", "pe_o", 2 * (ik - 1))
            t_st_in = s.dma("gpsimd", gin_i, st[:], st_in[:, ik * 128:(ik + 1) * 128])
            s.wait("vector", gin_i, t_st_in)
            if ik >= 2:
                s.wait("vector", "pe_o", 2 * (ik - 1))
            s.op("vector",
                 lambda E, sct=sct, st=st, kc=kc, c=c: E.scalar_tensor_tensor(
                     sct[:], S0[:, kc * 128:(kc + 1) * 128], gpcols[:, (c * 4 + kc):(c * 4 + kc) + 1],
                     st[:], ALU.mult, ALU.add),
                 "dve_sct")
            s.op("vector",
                 lambda E, qd2=qd2, kc=kc, csl=csl: E.tensor_tensor(
                     qd2[:], qrot[:, kc, csl], gamv_sb[:, kc, :], ALU.mult),
                 "dve_qd")
            for hh in range(2):
                h = 2 * kc + hh
                i = c * H + h
                pbase = hh * 64
                vt = (vTA, vTB)[(2 * ik + hh) % 2]
                gv_i = "gv" + str((2 * ik + hh) % 2)
                if 2 * ik + hh >= 2:
                    s.wait("gpsimd", "pe_o", 2 * ik + hh - 1)
                t_v_in = s.dma("gpsimd", gv_i, vt[:], vT_in[:, i * 128:(i + 1) * 128])
                s.op("tensor",
                     lambda E, pbase=pbase, kc=kc, csl=csl: E.matmul(
                         ps_sc[:], krot[pbase:pbase + 64, kc, csl], qrot[pbase:pbase + 64, kc, csl],
                         start=True, stop=True),
                     "pe_sc")
                msk = (mskA, mskB)[(2 * ik + hh) % 2]
                s.wait("vector", "pe_sc", s.cnt["pe_sc"])
                if 2 * ik + hh >= 2:
                    s.wait("vector", "pe_o", 2 * ik + hh - 1)
                s.op("vector",
                     lambda E, msk=msk, h=h: E.tensor_tensor(msk[:], ps_sc[:], maskT_sb[:, h, :], ALU.mult),
                     "dve_msk")
                s.wait("tensor", gv_i, t_v_in)
                s.wait("tensor", "dve_msk", s.cnt["dve_msk"])
                s.wait("tensor", "dve_qd", s.cnt["dve_qd"])
                s.wait("tensor", "dve_sct", s.cnt["dve_sct"])
                s.op("tensor",
                     lambda E, vt=vt, msk=msk: E.matmul(ps_o[:], vt[:], msk[:], start=True, stop=False))
                s.op("tensor",
                     lambda E, sct=sct, qd2=qd2, pbase=pbase: E.matmul(
                         ps_o[:], sct[pbase:pbase + 64, :], qd2[pbase:pbase + 64, :],
                         start=False, stop=True),
                     "pe_o")
                ob = (obA, obB)[(2 * ik + hh) % 2]
                s.wait("vector", "pe_o", s.cnt["pe_o"])
                if i >= 2 and (i - 2) in tr_marks:
                    s.wait("vector", "pe_T", tr_marks[i - 2])
                t_ob = s.op("vector", lambda E, ob=ob: E.tensor_copy(ob[:], ps_o[:]), "evac_o")
                s.wait("tensor", "evac_o", t_ob)
                t_tr, t_c = tp.run(ob[:], "vector",
                                   lambda E, i=i: E.tensor_copy(xnoT[:, i * 128:(i + 1) * 128], ps_T[:]),
                                   "dve_T")
                tr_marks[i] = t_tr
                s.op("vector",
                     lambda E, i=i: E.tensor_reduce(gso[:, i:i + 1], xnoT[:, i * 128:(i + 1) * 128],
                                                    AX.X, ALU.add))
                s.wait("scalar", "dve_T", t_c)
                s.op("scalar",
                     lambda E, i=i: E.activation(osq_junk[:], xnoT[:, i * 128:(i + 1) * 128],
                                                 AF.Square, accum_out=gsq[:, i:i + 1]),
                     "act_osq")
    t_oT = s.cnt["dve_T"]
    t_osq = s.cnt["act_osq"]

    s.wait("vector", "act_osq", t_osq)
    s.op("vector", lambda E: E.tensor_scalar(gm[:], gso[:], 1.0 / VH, None, ALU.mult))
    s.op("vector", lambda E: E.tensor_scalar(gv[:], gsq[:], 1.0 / VH, None, ALU.mult))
    s.op("vector", lambda E: E.tensor_tensor(gt1[:], gm[:], gm[:], ALU.mult))
    s.op("vector", lambda E: E.tensor_tensor(gv[:], gv[:], gt1[:], ALU.subtract))
    s.op("vector", lambda E: E.tensor_scalar(gv[:], gv[:], 1e-5, None, ALU.add))
    emit_rsqrt(s, gr[:], gv[:], gt1[:], gt2[:])

    last_tr = None
    for c in range(NCH):
        for h in range(H):
            i = c * H + h
            if last_tr is not None:
                s.wait("vector", "pe_T", last_tr)
            s.op("vector",
                 lambda E, i=i: E.tensor_scalar(onB[:], xnoT[:, i * 128:(i + 1) * 128],
                                                gm[:, i:i + 1], gr[:, i:i + 1],
                                                ALU.subtract, ALU.mult))
            s.op("vector", lambda E, h=h: E.tensor_tensor(onB[:], onB[:], gnwb[:, h, :], ALU.mult))
            s.op("vector", lambda E, h=h: E.tensor_tensor(onB[:], onB[:], gnbb[:, h, :], ALU.add))
            s.op("vector",
                 lambda E, i=i: E.tensor_tensor(onC[:], onB[:], sgT[:, i * 128:(i + 1) * 128], ALU.mult),
                 "dve_gate")
            s.wait("tensor", "dve_gate", s.cnt["dve_gate"])
            last_tr, _ = tp.run(onC[:], "vector",
                                lambda E, i=i: E.tensor_copy(sgT[:, i * 128:(i + 1) * 128], ps_T[:]),
                                "dve_T")
    t_gfm = s.cnt["dve_T"]

    s.wait("tensor", "dve_T", t_gfm)
    s.wait("tensor", "dve_w", t_wo)
    s.wait("scalar", "dve_T", t_gfm)
    for tt in range(4):
        for oc in range(8):
            j = tt * 8 + oc
            if j >= 1:
                s.wait("tensor", "evac_w", j)
            sgT4 = sgT[:].rearrange("p (g h f) -> p g h f", g=NCH, h=H)
            for fc in range(8):
                s.op("tensor",
                     lambda E, fc=fc, oc=oc, tt=tt: E.matmul(
                         ps_proj[:],
                         wqgb[:, fc, oc * 128:(oc + 1) * 128],
                         sgT4[:, tt * 4:(tt + 1) * 4, fc, :],
                         start=(fc == 0), stop=(fc == 7)),
                     "pe_wo" if fc == 7 else None)
            tpw = s.cnt["pe_wo"]
            s.wait("vector", "pe_wo", tpw)
            t_ev = s.op("vector",
                        lambda E, oc=oc, tt=tt: E.tensor_copy(woutb[:, oc, tt * 512:(tt + 1) * 512], ps_proj[:]),
                        "evac_w")
            ws = (wsqA, wsqB)[j % 2]
            s.wait("scalar", "evac_w", t_ev)
            if j >= 2:
                s.wait("scalar", "pe_wsq", j - 1)
            s.op("scalar",
                 lambda E, ws=ws, oc=oc, tt=tt: E.activation(
                     ws[:], woutb[:, oc, tt * 512:(tt + 1) * 512], AF.Square),
                 "act_wsq")
            if oc == 0 and tt >= 1:
                s.wait("tensor", "gq_wst", tt * 32)
            s.wait("tensor", "evac_w", j + 1)
            s.op("tensor",
                 lambda E, oc=oc, tt=tt: E.matmul(ps_sx[:], onescolb[:],
                                                  woutb[:, oc, tt * 512:(tt + 1) * 512],
                                                  start=(oc == 0), stop=(oc == 7)),
                 "pe_wst" if oc == 7 else None)
            s.wait("tensor", "act_wsq", s.cnt["act_wsq"])
            s.op("tensor",
                 lambda E, ws=ws, oc=oc: E.matmul(ps_sq[:], onescol[:], ws[:],
                                                  start=(oc == 0), stop=(oc == 7)),
                 "pe_wsq")
        s.wait("vector", "pe_wst", s.cnt["pe_wst"])
        s.wait("vector", "pe_wsq", s.cnt["pe_wsq"])
        if tt >= 1:
            s.wait("vector", "gq_wst", tt * 32)
        s.op("vector", lambda E: E.tensor_copy(srow2[0:1, 0:512], ps_sx[:]))
        s.op("vector", lambda E: E.tensor_copy(srow2[0:1, 512:1024], ps_sq[:]), "dve_wsr")
        s.wait("gpsimd", "dve_wsr", s.cnt["dve_wsr"])
        s.dma("gpsimd", "gq_wst", rows_scr[0:1, tt * 512:(tt + 1) * 512], srow2[0:1, 0:512])
        s.dma("gpsimd", "gq_wst", rows_scr[0:1, TLOC + tt * 512:TLOC + (tt + 1) * 512], srow2[0:1, 512:1024])

    t_rows = emit_stats_rows_to_sbuf(s, stat, rows_scr, wait_q="gq_wst", wait_v=s.cnt["gq_wst"])
    s.wait("vector", "gq", t_rows)
    t_bc = emit_stats_finish(s, "out", stat, sw, mb, rb, rows_scr, rowsb_scr, DM)

    for c in range(8):
        xr = (xresA, xresB)[c % 2]
        sx_i = "syx" + str(c % 2)
        if c >= 2:
            s.wait("sync", "dve_out", c - 1)
        s.dma("sync", sx_i, xr[:], xT.rearrange("(c p) t -> c p t", p=128)[c])
        s.wait("vector", sx_i, s.cnt[sx_i])
        if c == 0:
            s.wait("vector", "gq", t_bc)
        if c >= 1:
            s.wait("vector", "gq_out", c * 16)
        s.op("vector", lambda E, c=c: E.tensor_tensor(xout[:], woutb[:, c, :], mb[:], ALU.subtract))
        s.op("vector", lambda E: E.tensor_tensor(xout[:], xout[:], rb[:], ALU.mult))
        s.op("vector", lambda E, xr=xr: E.tensor_tensor(xout[:], xout[:], xr[:], ALU.add), "dve_out")
        s.wait("gpsimd", "dve_out", s.cnt["dve_out"])
        s.dma("gpsimd", "gq_out", x1_out.rearrange("(c p) t -> c p t", p=128)[c], xout[:])
    s.wait("gpsimd", "gq_out", s.cnt["gq_out"])
    s.wait("gpsimd", "gq", s.cnt["gq"])
    s.wait("gpsimd", "gin0", s.cnt["gin0"])
    s.wait("gpsimd", "gin1", s.cnt["gin1"])
    return s.build()


def build_l0f():
    s = Sched("l0f")
    nc = s.nc
    xTe = nc.dram_tensor("xTe", [DM, TLOC + 1], F32, kind="ExternalInput").ap()
    wrkT = nc.dram_tensor("wrkT", [DM, 2048], F32, kind="ExternalInput").ap()
    wvT2 = nc.dram_tensor("wvT2", [DM, DM], F32, kind="ExternalInput").ap()
    mucolsd = nc.dram_tensor("mucols", [128, 16], F32, kind="ExternalInput").ap()
    onesd = nc.dram_tensor("ones", [128, 1], F32, kind="ExternalInput").ap()
    x2_out = nc.dram_tensor("x2_out", [DM, TLOC], F32, kind="ExternalOutput").ap()
    rows_scr = nc.dram_tensor("rows_scr", [1, 2 * TLOC + 2], F32, kind="Internal").ap()
    rowsb_scr = nc.dram_tensor("rowsb_scr", [1, 2 * TLOC + 2], BF16, kind="Internal").ap()

    A = nc.alloc_sbuf_tensor
    xin = A("xin", [128, 8, TLOC + 1], F32)
    wrkb = A("wrkb", [128, 8, 2048], BF16)
    wvb = A("wvb", [128, 8, 1024], BF16)
    xrkA = A("xrkA", [128, 8, 1024], BF16)
    xrkB = xrkA
    rT = A("rT", [128, 8, 512], BF16)
    kact = A("kact", [128, 8, 512], BF16)
    fout = A("fout", [128, 8, TLOC], BF16)
    nrmA = A("nrmA", [128, 513], F32)
    nrmB = nrmA
    dif = A("dif", [128, 512], F32)
    ktmp = A("ktmp", [128, 512], F32)
    wtmpA = A("wtmpA", [128, 512], F32)
    wtmpB = A("wtmpB", [128, 512], F32)
    wsqA = A("wsqA", [128, 512], F32)
    wsqB = A("wsqB", [128, 512], F32)
    mucols = A("mucols_sb", [128, 16], F32)
    onescol = A("onescol", [128, 1], F32)
    onescolb = A("onescolb", [128, 1], BF16)
    mbe = A("mbe", [128, TLOC + 1], BF16)
    rbe = A("rbe", [128, TLOC + 1], BF16)
    mb = mbe[:, 0:TLOC]
    rb = rbe[:, 0:TLOC]
    stat = A("stat", [128, 36], F32)
    sw = {k: A(k, [128, 17], F32) for k in ("stm", "stv", "str_", "stt1", "stt2")}
    sw["mrow"] = A("mrow", [128, 17], BF16)
    sw["rrow"] = A("rrow", [128, 17], BF16)
    srow2 = A("srow2", [1, 1024], F32)
    sq0 = A("sq0", [128, 8], F32)
    b0 = A("b0", [1, 2], F32)
    xout = A("xout", [128, 512], F32)

    P = nc.alloc_psum_tensor
    ps_proj = P("ps_proj", [128, 512], F32)
    ps_sx = P("ps_sx", [1, 512], F32)
    ps_sq = P("ps_sq", [1, 512], F32)
    ps_b0 = P("ps_b0", [1, 2], F32)

    s.dma("sync", "sy_x", xin[:], xTe.rearrange("(c p) t -> p c t", p=128))
    s.dma("sync", "sy_c", mucols[:], mucolsd)
    s.dma("sync", "sy_c", onescol[:], onesd)
    s.dma("gpsimd", "gq_b", onescolb[:], onesd)
    wrk_r = wrkT.rearrange("(c p) (u o) -> c u p o", p=128, o=512)
    for cu in range(32):
        wt = (wtmpA, wtmpB)[cu % 2]
        wq = "sy_w" + str(cu % 2)
        if cu >= 2:
            s.wait("sync", "dve_w", cu - 1)
        s.dma("sync", wq, wt[:], wrk_r[cu // 4, cu % 4])
        s.wait("vector", wq, s.cnt[wq])
        c, u = cu // 4, cu % 4
        s.op("vector",
             lambda E, c=c, u=u, wt=wt: E.tensor_copy(wrkb[:, c, u * 512:(u + 1) * 512], wt[:]),
             "dve_w")
    wv_r = wvT2.rearrange("(c p) (u o) -> c u p o", p=128, o=512)
    for cu in range(16):
        wt = (wtmpA, wtmpB)[cu % 2]
        wq = "sy_w" + str(cu % 2)
        s.wait("sync", "dve_w", 31 + cu)
        s.dma("sync", wq, wt[:], wv_r[cu // 2, cu % 2])
        s.wait("vector", wq, s.cnt[wq])
        c, u = cu // 2, cu % 2
        s.op("vector",
             lambda E, c=c, u=u, wt=wt: E.tensor_copy(wvb[:, c, u * 512:(u + 1) * 512], wt[:]),
             "dve_w")

    s.wait("scalar", "sy_x", 16)
    s.wait("tensor", "sy_x", 16)
    s.wait("tensor", "sy_c", s.cnt["sy_c"])
    xin_main = xin[:].rearrange("p c t -> p c t")
    for tt in range(4):
        if tt >= 1:
            s.wait("tensor", "gq_st", tt * 32)
        for c in range(8):
            k = tt * 8 + c
            xs = (wsqA, wsqB)[k % 2]
            if k >= 2:
                s.wait("scalar", "pe_sq", k - 1)
            s.op("scalar",
                 lambda E, xs=xs, c=c, tt=tt: E.activation(
                     xs[:], xin[:, c, 1 + tt * 512:1 + (tt + 1) * 512], AF.Square),
                 "act_sq")
            s.op("tensor",
                 lambda E, c=c, tt=tt: E.matmul(ps_sx[:], onescol[:],
                                                xin[:, c, 1 + tt * 512:1 + (tt + 1) * 512],
                                                start=(c == 0), stop=(c == 7)),
                 "pe_st" if c == 7 else None)
            s.wait("tensor", "act_sq", k + 1)
            s.op("tensor",
                 lambda E, xs=xs, c=c: E.matmul(ps_sq[:], onescol[:], xs[:],
                                                start=(c == 0), stop=(c == 7)),
                 "pe_sq")
        s.wait("vector", "pe_st", s.cnt["pe_st"])
        s.wait("vector", "pe_sq", s.cnt["pe_sq"])
        if tt >= 1:
            s.wait("vector", "gq_st", tt * 32)
        s.op("vector", lambda E: E.tensor_copy(srow2[0:1, 0:512], ps_sx[:]))
        s.op("vector", lambda E: E.tensor_copy(srow2[0:1, 512:1024], ps_sq[:]), "dve_sr")
        s.wait("gpsimd", "dve_sr", s.cnt["dve_sr"])
        s.dma("gpsimd", "gq_st", rows_scr[0:1, tt * 512:(tt + 1) * 512], srow2[0:1, 0:512])
        s.dma("gpsimd", "gq_st", rows_scr[0:1, TLOC + tt * 512:TLOC + (tt + 1) * 512], srow2[0:1, 512:1024])
    s.op("scalar", lambda E: E.activation(sq0[:], xin[:, :, 0], AF.Square), "act_sq0")
    for c in range(8):
        s.op("tensor",
             lambda E, c=c: E.matmul(ps_b0[0:1, 0:1], onescol[:], xin[:, c, 0:1],
                                     start=(c == 0), stop=(c == 7)))
    s.wait("tensor", "act_sq0", 1)
    for c in range(8):
        s.op("tensor",
             lambda E, c=c: E.matmul(ps_b0[0:1, 1:2], onescol[:], sq0[:, c:c + 1],
                                     start=(c == 0), stop=(c == 7)),
             "pe_b0" if c == 7 else None)
    s.wait("vector", "pe_b0", s.cnt["pe_b0"])
    s.op("vector", lambda E: E.tensor_copy(b0[:], ps_b0[:]), "dve_b0")
    s.wait("gpsimd", "dve_b0", s.cnt["dve_b0"])
    s.dma("gpsimd", "gq_st", rows_scr[0:1, 2 * TLOC:2 * TLOC + 2], b0[:])
    s.op("vector", lambda E: E.memset(stat[:], 1.0), "dve_stz")
    s.wait("gpsimd", "gq_st", s.cnt["gq_st"])
    s.wait("gpsimd", "dve_stz", 1)
    s.dma("gpsimd", "gq", stat[:, 0:16], rows_scr[0:1, 0:TLOC].rearrange("a (p x) -> (a p) x", p=128))
    s.dma("gpsimd", "gq", stat[:, 17:33], rows_scr[0:1, TLOC:2 * TLOC].rearrange("a (p x) -> (a p) x", p=128))
    s.dma("gpsimd", "gq", stat[0:1, 16:17], rows_scr[0:1, 2 * TLOC:2 * TLOC + 1])
    t_st = s.dma("gpsimd", "gq", stat[0:1, 33:34], rows_scr[0:1, 2 * TLOC + 1:2 * TLOC + 2])
    s.wait("vector", "gq", t_st)
    f = 17
    s.op("vector", lambda E: E.tensor_scalar(sw["stm"][:], stat[:, 0:17], 1.0 / DM, None, ALU.mult))
    s.op("vector", lambda E: E.tensor_scalar(sw["stv"][:], stat[:, 17:34], 1.0 / DM, None, ALU.mult))
    s.op("vector", lambda E: E.tensor_tensor(sw["stt1"][:], sw["stm"][:], sw["stm"][:], ALU.mult))
    s.op("vector", lambda E: E.tensor_tensor(sw["stv"][:], sw["stv"][:], sw["stt1"][:], ALU.subtract))
    s.op("vector", lambda E: E.tensor_scalar(sw["stv"][:], sw["stv"][:], 1e-5, None, ALU.add))
    emit_rsqrt(s, sw["str_"][:], sw["stv"][:], sw["stt1"][:], sw["stt2"][:])
    s.op("vector", lambda E: E.tensor_copy(sw["mrow"][:], sw["stm"][:]))
    s.op("vector", lambda E: E.tensor_copy(sw["rrow"][:], sw["str_"][:]), "dve_st2")
    s.wait("gpsimd", "dve_st2", s.cnt["dve_st2"])
    s.dma("gpsimd", "gq", rowsb_scr[0:1, 1:TLOC + 1].rearrange("a (p x) -> (a p) x", p=128), sw["mrow"][:, 0:16])
    s.dma("gpsimd", "gq", rowsb_scr[0:1, 0:1], sw["mrow"][0:1, 16:17])
    s.dma("gpsimd", "gq", rowsb_scr[0:1, TLOC + 2:2 * TLOC + 2].rearrange("a (p x) -> (a p) x", p=128),
          sw["rrow"][:, 0:16])
    s.dma("gpsimd", "gq", rowsb_scr[0:1, TLOC + 1:TLOC + 2], sw["rrow"][0:1, 16:17])
    s.wait("gpsimd", "gq", s.cnt["gq"])
    s.dma("gpsimd", "gq", mbe[:], rowsb_scr[0:1, 0:TLOC + 1].broadcast_to((128, TLOC + 1)))
    t_bc = s.dma("gpsimd", "gq", rbe[:], rowsb_scr[0:1, TLOC + 1:2 * TLOC + 2].broadcast_to((128, TLOC + 1)))

    s.wait("vector", "gq", t_bc)
    s.wait("vector", "dve_w", 48)
    s.wait("tensor", "dve_w", 48)
    for tt in range(4):
        xrk = (xrkA, xrkB)[tt % 2]
        for c in range(8):
            nb_ = (nrmA, nrmB)[c % 2]
            esl = slice(tt * 512, tt * 512 + 513)
            if tt >= 1:
                s.wait("vector", "pe_rk", tt * 24)
            s.op("vector", lambda E, c=c, esl=esl, nb_=nb_: E.tensor_tensor(
                nb_[:], xin[:, c, esl], mbe[:, esl], ALU.subtract))
            s.op("vector", lambda E, c=c, esl=esl, nb_=nb_: E.tensor_tensor(
                nb_[:], nb_[:], rbe[:, esl], ALU.mult))
            s.op("vector", lambda E, nb_=nb_: E.tensor_tensor(
                dif[:], nb_[:, 1:513], nb_[:, 0:512], ALU.subtract))
            s.op("vector", lambda E, c=c, nb_=nb_, xrk=xrk: E.scalar_tensor_tensor(
                xrk[:, c, 0:512], dif[:], mucols[:, c:c + 1], nb_[:, 0:512], ALU.mult, ALU.add))
            s.op("vector", lambda E, c=c, nb_=nb_, xrk=xrk: E.scalar_tensor_tensor(
                xrk[:, c, 512:1024], dif[:], mucols[:, 8 + c:9 + c], nb_[:, 0:512], ALU.mult, ALU.add),
                 "dve_rk")
        s.wait("tensor", "dve_rk", s.cnt["dve_rk"])
        for oc in range(8):
            j2 = tt * 24 + oc
            if j2 >= 1:
                s.wait("tensor", "evac", j2)
            for c in range(8):
                s.op("tensor",
                     lambda E, oc=oc, c=c, xrk=xrk: E.matmul(
                         ps_proj[:], wrkb[:, c, oc * 128:(oc + 1) * 128], xrk[:, c, 0:512],
                         start=(c == 0), stop=(c == 7)),
                     "pe_rk" if c == 7 else None)
            s.wait("scalar", "pe_rk", s.cnt["pe_rk"])
            s.op("scalar",
                 lambda E, oc=oc: E.activation(rT[:, oc, :], ps_proj[:], AF.Sigmoid), "evac")
        for oc in range(8):
            j2 = tt * 24 + 8 + oc
            s.wait("tensor", "evac", j2)
            for c in range(8):
                s.op("tensor",
                     lambda E, oc=oc, c=c, xrk=xrk: E.matmul(
                         ps_proj[:], wrkb[:, c, 1024 + oc * 128:1024 + (oc + 1) * 128], xrk[:, c, 512:1024],
                         start=(c == 0), stop=(c == 7)),
                     "pe_rk" if c == 7 else None)
            s.wait("vector", "pe_rk", s.cnt["pe_rk"])
            s.wait("vector", "act_ka", s.cnt["act_ka"] if "act_ka" in s.marks else 0)
            t_kr = s.op("vector",
                        lambda E: E.tensor_scalar(ktmp[:], ps_proj[:], 0.0, None, ALU.max), "evac")
            s.wait("scalar", "evac", t_kr)
            s.op("scalar",
                 lambda E, oc=oc: E.activation(kact[:, oc, :], ktmp[:], AF.Square), "act_ka")
        s.wait("tensor", "act_ka", s.cnt["act_ka"])
        for oc in range(8):
            j2 = tt * 24 + 16 + oc
            s.wait("tensor", "evac", j2)
            for c in range(8):
                s.op("tensor",
                     lambda E, oc=oc, c=c: E.matmul(
                         ps_proj[:], wvb[:, c, oc * 128:(oc + 1) * 128], kact[:, c, :],
                         start=(c == 0), stop=(c == 7)),
                     "pe_rk" if c == 7 else None)
            s.wait("vector", "pe_rk", s.cnt["pe_rk"])
            s.op("vector",
                 lambda E, oc=oc, tt=tt: E.tensor_tensor(
                     fout[:, oc, tt * 512:(tt + 1) * 512], ps_proj[:], rT[:, oc, :], ALU.mult),
                 "evac")
    t_fout = s.cnt["evac"]

    s.wait("scalar", "evac", t_fout)
    s.wait("tensor", "evac", t_fout)
    s.wait("tensor", "gq_b", s.cnt["gq_b"])
    for tt in range(4):
        for oc in range(8):
            j = tt * 8 + oc
            ws = (wsqA, wsqB)[j % 2]
            if j >= 2:
                s.wait("scalar", "pe_wsq", j - 1)
            s.op("scalar",
                 lambda E, ws=ws, oc=oc, tt=tt: E.activation(
                     ws[:], fout[:, oc, tt * 512:(tt + 1) * 512], AF.Square),
                 "act_wsq")
            if oc == 0 and tt >= 1:
                s.wait("tensor", "gq_wst", tt * 32)
            s.op("tensor",
                 lambda E, oc=oc, tt=tt: E.matmul(ps_sx[:], onescolb[:],
                                                  fout[:, oc, tt * 512:(tt + 1) * 512],
                                                  start=(oc == 0), stop=(oc == 7)),
                 "pe_wst" if oc == 7 else None)
            s.wait("tensor", "act_wsq", s.cnt["act_wsq"])
            s.op("tensor",
                 lambda E, ws=ws, oc=oc: E.matmul(ps_sq[:], onescol[:], ws[:],
                                                  start=(oc == 0), stop=(oc == 7)),
                 "pe_wsq")
        s.wait("vector", "pe_wst", s.cnt["pe_wst"])
        s.wait("vector", "pe_wsq", s.cnt["pe_wsq"])
        if tt >= 1:
            s.wait("vector", "gq_wst", tt * 32)
        s.op("vector", lambda E: E.tensor_copy(srow2[0:1, 0:512], ps_sx[:]))
        s.op("vector", lambda E: E.tensor_copy(srow2[0:1, 512:1024], ps_sq[:]), "dve_wsr")
        s.wait("gpsimd", "dve_wsr", s.cnt["dve_wsr"])
        s.dma("gpsimd", "gq_wst", rows_scr[0:1, tt * 512:(tt + 1) * 512], srow2[0:1, 0:512])
        s.dma("gpsimd", "gq_wst", rows_scr[0:1, TLOC + tt * 512:TLOC + (tt + 1) * 512], srow2[0:1, 512:1024])
    t_rows = emit_stats_rows_to_sbuf(s, stat, rows_scr, wait_q="gq_wst", wait_v=s.cnt["gq_wst"])
    s.wait("vector", "gq", t_rows)
    t_bc2 = emit_stats_finish(s, "out", stat, sw, mb, rb, rows_scr, rowsb_scr, DM)
    s.wait("vector", "gq", t_bc2)
    x2r = x2_out.rearrange("(c p) (u t) -> c u p t", p=128, t=512)
    for c in range(8):
        for u in range(4):
            cu = c * 4 + u
            sl = slice(u * 512, (u + 1) * 512)
            if cu >= 1:
                s.wait("vector", "gq_out", cu * 16)
            s.op("vector", lambda E, c=c, sl=sl: E.tensor_tensor(xout[:], fout[:, c, sl], mb[:, sl], ALU.subtract))
            s.op("vector", lambda E, sl=sl: E.tensor_tensor(xout[:], xout[:], rb[:, sl], ALU.mult))
            s.op("vector", lambda E, c=c, sl=sl: E.tensor_tensor(
                xout[:], xout[:], xin[:, c, 1 + sl.start:1 + sl.stop], ALU.add), "dve_out")
            s.wait("gpsimd", "dve_out", s.cnt["dve_out"])
            s.dma("gpsimd", "gq_out", x2r[c, u], xout[:])
    s.wait("gpsimd", "gq_out", s.cnt["gq_out"])
    s.wait("gpsimd", "gq", s.cnt["gq"])
    return s.build()


def build_l1a_pre():
    s = Sched("l1a_pre")
    nc = s.nc
    TE = TLOC + 3
    xTe = nc.dram_tensor("xTe", [DM, TE], F32, kind="ExternalInput").ap()
    winT = nc.dram_tensor("winT", [DM, 4096], F32, kind="ExternalInput").ap()
    xpT = nc.dram_tensor("xpT", [2048, 33], F32, kind="ExternalInput").ap()
    wconvd = nc.dram_tensor("wconv", [128, 16 * 4], F32, kind="ExternalInput").ap()
    cbcold = nc.dram_tensor("cbcol", [128, 16], F32, kind="ExternalInput").ap()
    onesd = nc.dram_tensor("ones", [128, 1], F32, kind="ExternalInput").ap()
    xcb_out = nc.dram_tensor("xcb_out", [2048, TLOC], BF16, kind="ExternalOutput").ap()
    zsb_out = nc.dram_tensor("zsb_out", [2048, TLOC], BF16, kind="ExternalOutput").ap()
    bcdt_out = nc.dram_tensor("bcdt_out", [33, TLOC], F32, kind="ExternalOutput").ap()
    rows_scr = nc.dram_tensor("rows_scr", [1, 2 * TLOC + 6], F32, kind="Internal").ap()
    rowsb_scr = nc.dram_tensor("rowsb_scr", [1, 2 * TLOC + 6], BF16, kind="Internal").ap()

    A = nc.alloc_sbuf_tensor
    xin = A("xin", [128, 8, TE], F32)
    xne = A("xne", [128, 8, TE], BF16)
    wtA = A("wtA", [128, 8, 128], F32)
    wtB = A("wtB", [128, 8, 128], F32)
    wbA = A("wbA", [128, 8, 128], BF16)
    wbB = A("wbB", [128, 8, 128], BF16)
    xig = A("xig", [128, TE], F32)
    xib = A("xib", [128, TE], BF16)
    xcg = A("xcg", [128, TLOC], F32)
    xcb = A("xcb", [128, TLOC], BF16)
    xpb = A("xpb", [128, 16, 33], BF16)
    wconv = A("wconv_sb", [128, 16, 4], F32)
    cbcol = A("cbcol_sb", [128, 16], F32)
    onescol = A("onescol", [128, 1], F32)
    sq0 = A("sq0", [128, 8, 3], F32)
    b0 = A("b0", [1, 6], F32)
    mbe = A("mbe", [128, TE], BF16)
    rbe = A("rbe", [128, TE], BF16)
    stat = A("stat", [128, 40], F32)
    sw = {k: A(k, [128, 19], F32) for k in ("stm", "stv", "str_", "stt1", "stt2")}
    sw["mrow"] = A("mrow", [128, 19], BF16)
    sw["rrow"] = A("rrow", [128, 19], BF16)
    srow2 = A("srow2", [1, 1024], F32)
    wsqA = A("wsqA", [128, 512], F32)
    wsqB = A("wsqB", [128, 512], F32)
    bc_ev = A("bc_ev", [33, 512], F32)

    P = nc.alloc_psum_tensor
    ps_proj = P("ps_proj", [128, 512], F32)
    ps_sx = P("ps_sx", [1, 512], F32)
    ps_sq = P("ps_sq", [1, 512], F32)
    ps_bc = [P("ps_bc%d" % t, [33, 512], F32) for t in range(4)]
    ps_pe = ps_proj[:, 0:3]
    ps_b0 = ps_sx[0:1, 0:6]

    s.dma("sync", "sy_x", xin[:], xTe.rearrange("(c p) t -> p c t", p=128))
    s.dma("sync", "sy_c", wconv[:], wconvd.rearrange("p (g j) -> p g j", j=4))
    s.dma("sync", "sy_c", cbcol[:], cbcold)
    s.dma("sync", "sy_c", onescol[:], onesd)
    s.dma("gpsimd", "gq_b", xpb[:], xpT.rearrange("(g p) o -> p g o", p=128))

    s.wait("scalar", "sy_x", 16)
    s.wait("tensor", "sy_x", 16)
    s.wait("tensor", "sy_c", s.cnt["sy_c"])
    for tt in range(4):
        if tt >= 1:
            s.wait("tensor", "gq_st", tt * 32)
        for c in range(8):
            k = tt * 8 + c
            xs = (wsqA, wsqB)[k % 2]
            if k >= 2:
                s.wait("scalar", "pe_sq", k - 1)
            s.op("scalar",
                 lambda E, xs=xs, c=c, tt=tt: E.activation(
                     xs[:], xin[:, c, 3 + tt * 512:3 + (tt + 1) * 512], AF.Square),
                 "act_sq")
            s.op("tensor",
                 lambda E, c=c, tt=tt: E.matmul(ps_sx[:], onescol[:],
                                                xin[:, c, 3 + tt * 512:3 + (tt + 1) * 512],
                                                start=(c == 0), stop=(c == 7)),
                 "pe_st" if c == 7 else None)
            s.wait("tensor", "act_sq", k + 1)
            s.op("tensor",
                 lambda E, xs=xs, c=c: E.matmul(ps_sq[:], onescol[:], xs[:],
                                                start=(c == 0), stop=(c == 7)),
                 "pe_sq")
        s.wait("vector", "pe_st", s.cnt["pe_st"])
        s.wait("vector", "pe_sq", s.cnt["pe_sq"])
        if tt >= 1:
            s.wait("vector", "gq_st", tt * 32)
        s.op("vector", lambda E: E.tensor_copy(srow2[0:1, 0:512], ps_sx[:]))
        s.op("vector", lambda E: E.tensor_copy(srow2[0:1, 512:1024], ps_sq[:]), "dve_sr")
        s.wait("gpsimd", "dve_sr", s.cnt["dve_sr"])
        s.dma("gpsimd", "gq_st", rows_scr[0:1, tt * 512:(tt + 1) * 512], srow2[0:1, 0:512])
        s.dma("gpsimd", "gq_st", rows_scr[0:1, TLOC + tt * 512:TLOC + (tt + 1) * 512], srow2[0:1, 512:1024])
    s.op("scalar", lambda E: E.activation(sq0[:], xin[:, :, 0:3], AF.Square), "act_sq0")
    s.wait("tensor", "gq_st", s.cnt["gq_st"])
    for c in range(8):
        s.op("tensor",
             lambda E, c=c: E.matmul(ps_b0[0:1, 0:3], onescol[:], xin[:, c, 0:3],
                                     start=(c == 0), stop=(c == 7)))
    s.wait("tensor", "act_sq0", 1)
    for c in range(8):
        s.op("tensor",
             lambda E, c=c: E.matmul(ps_b0[0:1, 3:6], onescol[:], sq0[:, c, :],
                                     start=(c == 0), stop=(c == 7)),
             "pe_b0" if c == 7 else None)
    s.wait("vector", "pe_b0", s.cnt["pe_b0"])
    s.op("vector", lambda E: E.tensor_copy(b0[:], ps_b0), "dve_b0")
    s.wait("gpsimd", "dve_b0", s.cnt["dve_b0"])
    s.dma("gpsimd", "gq_st", rows_scr[0:1, 2 * TLOC:2 * TLOC + 6], b0[:])
    s.op("vector", lambda E: E.memset(stat[:], 1.0), "dve_stz")
    s.wait("gpsimd", "gq_st", s.cnt["gq_st"])
    s.wait("gpsimd", "dve_stz", 1)
    s.dma("gpsimd", "gq", stat[:, 0:16], rows_scr[0:1, 0:TLOC].rearrange("a (p x) -> (a p) x", p=128))
    s.dma("gpsimd", "gq", stat[:, 19:35], rows_scr[0:1, TLOC:2 * TLOC].rearrange("a (p x) -> (a p) x", p=128))
    s.dma("gpsimd", "gq", stat[0:1, 16:19], rows_scr[0:1, 2 * TLOC:2 * TLOC + 3])
    t_st = s.dma("gpsimd", "gq", stat[0:1, 35:38], rows_scr[0:1, 2 * TLOC + 3:2 * TLOC + 6])
    s.wait("vector", "gq", t_st)
    s.op("vector", lambda E: E.tensor_scalar(sw["stm"][:], stat[:, 0:19], 1.0 / DM, None, ALU.mult))
    s.op("vector", lambda E: E.tensor_scalar(sw["stv"][:], stat[:, 19:38], 1.0 / DM, None, ALU.mult))
    s.op("vector", lambda E: E.tensor_tensor(sw["stt1"][:], sw["stm"][:], sw["stm"][:], ALU.mult))
    s.op("vector", lambda E: E.tensor_tensor(sw["stv"][:], sw["stv"][:], sw["stt1"][:], ALU.subtract))
    s.op("vector", lambda E: E.tensor_scalar(sw["stv"][:], sw["stv"][:], 1e-5, None, ALU.add))
    emit_rsqrt(s, sw["str_"][:], sw["stv"][:], sw["stt1"][:], sw["stt2"][:])
    s.op("vector", lambda E: E.tensor_copy(sw["mrow"][:], sw["stm"][:]))
    s.op("vector", lambda E: E.tensor_copy(sw["rrow"][:], sw["str_"][:]), "dve_st2")
    s.wait("gpsimd", "dve_st2", s.cnt["dve_st2"])
    s.dma("gpsimd", "gq", rowsb_scr[0:1, 3:TLOC + 3].rearrange("a (p x) -> (a p) x", p=128), sw["mrow"][:, 0:16])
    s.dma("gpsimd", "gq", rowsb_scr[0:1, 0:3], sw["mrow"][0:1, 16:19])
    s.dma("gpsimd", "gq", rowsb_scr[0:1, TE + 3:2 * TE].rearrange("a (p x) -> (a p) x", p=128), sw["rrow"][:, 0:16])
    s.dma("gpsimd", "gq", rowsb_scr[0:1, TE:TE + 3], sw["rrow"][0:1, 16:19])
    s.wait("gpsimd", "gq", s.cnt["gq"])
    s.dma("gpsimd", "gq", mbe[:], rowsb_scr[0:1, 0:TE].broadcast_to((128, TE)))
    t_bc = s.dma("gpsimd", "gq", rbe[:], rowsb_scr[0:1, TE:2 * TE].broadcast_to((128, TE)))
    s.wait("vector", "gq", t_bc)
    ranges = [(0, 512), (512, 1024), (1024, 1536), (1536, 2048), (2048, TE)]
    for c in range(8):
        for (a, b_) in ranges:
            sl = slice(a, b_)
            s.op("vector", lambda E, c=c, sl=sl: E.tensor_tensor(wsqA[:, 0:sl.stop - sl.start],
                                                                 xin[:, c, sl], mbe[:, sl], ALU.subtract))
            s.op("vector", lambda E, c=c, sl=sl: E.tensor_tensor(xne[:, c, sl], wsqA[:, 0:sl.stop - sl.start],
                                                                 rbe[:, sl], ALU.mult),
                 "dve_xn" if (c == 7 and sl.stop == TE) else None)
    t_xn = s.cnt["dve_xn"]

    s.wait("tensor", "dve_xn", t_xn)
    s.wait("tensor", "gq_b", s.cnt["gq_b"])
    win_r = winT.rearrange("(c p) (g o) -> g c p o", p=128, o=128)
    for g in range(32):
        wt = (wtA, wtB)[g % 2]
        wb = (wbA, wbB)[g % 2]
        wq = "sy_w" + str(g % 2)
        if g >= 2:
            s.wait("sync", "dve_w", (g - 1) * 8)
        for c in range(8):
            s.dma("sync", wq, wt[:, c, :], win_r[g, c])
        s.wait("vector", wq, s.cnt[wq])
        for c in range(8):
            s.op("vector", lambda E, wb=wb, wt=wt, c=c: E.tensor_copy(wb[:, c, :], wt[:, c, :]), "dve_w")
        s.wait("tensor", "dve_w", s.cnt["dve_w"])
        if g >= 1:
            s.wait("tensor", "dve_ev", s.cnt["dve_ev"])
        for u in range(4):
            if s.cnt["dve_ev"] if "dve_ev" in s.marks else 0:
                s.wait("tensor", "dve_ev", s.cnt["dve_ev"])
            for c in range(8):
                s.op("tensor",
                     lambda E, wb=wb, c=c, u=u: E.matmul(
                         ps_proj[:], wb[:, c, :], xne[:, c, 3 + u * 512:3 + (u + 1) * 512],
                         start=(c == 0), stop=(c == 7)),
                     "pe_pj" if c == 7 else None)
            s.wait("vector", "pe_pj", s.cnt["pe_pj"])
            if g < 16:
                s.op("vector", lambda E, u=u: E.tensor_copy(xig[:, 3 + u * 512:3 + (u + 1) * 512], ps_proj[:]),
                     "dve_ev")
            else:
                s.wait("scalar", "pe_pj", s.cnt["pe_pj"])
                if u == 0:
                    if g == 16:
                        s.wait("scalar", "gq_xc", 16 * 16)
                    elif g >= 17:
                        s.wait("scalar", "gq_z", (g - 16) * 16)
                s.op("scalar",
                     lambda E, u=u: E.activation(xcb[:, u * 512:(u + 1) * 512], ps_proj[:], AF.Silu),
                     "dve_ev")
        if g < 16:
            s.wait("tensor", "dve_ev", s.cnt["dve_ev"])
            for c in range(8):
                s.op("tensor",
                     lambda E, wb=wb, c=c: E.matmul(ps_pe, wb[:, c, :], xne[:, c, 0:3],
                                                    start=(c == 0), stop=(c == 7)),
                     "pe_pj" if c == 7 else None)
            s.wait("vector", "pe_pj", s.cnt["pe_pj"])
            s.op("vector", lambda E: E.tensor_copy(xig[:, 0:3], ps_pe), "dve_ev")
            s.op("vector",
                 lambda E, g=g: E.tensor_scalar(xcg[:], xig[:, 0:TLOC], wconv[:, g, 0:1],
                                                cbcol[:, g:g + 1], ALU.mult, ALU.add))
            for j in range(1, 4):
                s.op("vector",
                     lambda E, g=g, j=j: E.scalar_tensor_tensor(
                         xcg[:], xig[:, j:j + TLOC], wconv[:, g, j:j + 1], xcg[:], ALU.mult, ALU.add),
                     "dve_cv" if j == 3 else None)
            s.wait("scalar", "dve_cv", s.cnt["dve_cv"])
            if g >= 1:
                s.wait("scalar", "gq_xc", g * 16)
            s.op("scalar", lambda E: E.activation(xcb[:], xcg[:], AF.Silu), "act_xc")
            s.wait("tensor", "act_xc", s.cnt["act_xc"])
            for u in range(4):
                s.op("tensor",
                     lambda E, g=g, u=u: E.matmul(ps_bc[u][:], xpb[:, g, :],
                                                  xcb[:, u * 512:(u + 1) * 512],
                                                  start=(g == 0), stop=(g == 15)),
                     "pe_bc" if (u == 3 and g == 15) else None)
            s.wait("gpsimd", "act_xc", s.cnt["act_xc"])
            if g >= 1:
                s.wait("gpsimd", "gq_xc", (g - 1) * 16)
            s.dma("gpsimd", "gq_xc", xcb_out.rearrange("(g p) t -> g p t", p=128)[g], xcb[:])
        else:
            s.wait("gpsimd", "dve_ev", s.cnt["dve_ev"])
            if g >= 17:
                s.wait("gpsimd", "gq_z", (g - 17) * 16)
            s.dma("gpsimd", "gq_z", zsb_out.rearrange("(g p) t -> g p t", p=128)[g - 16], xcb[:])
    s.wait("vector", "pe_bc", s.cnt["pe_bc"])
    for u in range(4):
        s.op("vector", lambda E, u=u: E.tensor_copy(bc_ev[:], ps_bc[u][:]), "dve_bc")
        s.wait("gpsimd", "dve_bc", s.cnt["dve_bc"])
        s.dma("gpsimd", "gq", bcdt_out[:, u * 512:(u + 1) * 512], bc_ev[:])
        s.wait("vector", "gq", s.cnt["gq"])
    s.wait("gpsimd", "gq", s.cnt["gq"])
    s.wait("gpsimd", "gq_xc", s.cnt["gq_xc"])
    s.wait("gpsimd", "gq_z", s.cnt["gq_z"])
    return s.build()


def build_l1a_scan():
    s = Sched("l1a_scan")
    nc = s.nc
    TG = B * T
    xcd = nc.dram_tensor("xcd", [256, TG], BF16, kind="ExternalInput").ap()
    bcdtd = nc.dram_tensor("bcdtd", [33, TG], F32, kind="ExternalInput").ap()
    dtwd = nc.dram_tensor("dtw", [128, 2], F32, kind="ExternalInput").ap()
    dtbd = nc.dram_tensor("dtb", [128, 2], F32, kind="ExternalInput").ap()
    alogd = nc.dram_tensor("alog", [128, 2 * 16], F32, kind="ExternalInput").ap()
    dcold = nc.dram_tensor("dcol", [128, 2], F32, kind="ExternalInput").ap()
    y_out = nc.dram_tensor("y_out", [256, TG], BF16, kind="ExternalOutput").ap()

    SEG = 2048
    NSEGS = TG // SEG
    A = nc.alloc_sbuf_tensor
    xcs = A("xcs", [128, SEG], BF16)
    dtp = A("dtp", [128, SEG], F32)
    dts = A("dts", [128, SEG], F32)
    ut = A("ut", [128, SEG], F32)
    at = A("at", [128, SEG], F32)
    bt = A("bt", [128, SEG], F32)
    ht = A("ht", [128, SEG], F32)
    tmp = A("tmp", [128, SEG], F32)
    yt = A("yt", [128, SEG], F32)
    yo = A("yo", [128, SEG], BF16)
    dtraw_b = A("dtraw_b", [128, SEG], F32)
    bbcA = A("bbcA", [128, SEG], F32)
    bbcB = A("bbcB", [128, SEG], F32)
    cbcA = A("cbcA", [128, SEG], F32)
    cbcB = A("cbcB", [128, SEG], F32)
    sp1 = A("sp1", [128, SEG], F32)
    dtw = A("dtw_sb", [128, 2], F32)
    dtb = A("dtb_sb", [128, 2], F32)
    alog = A("alog_sb", [128, 2, 16], F32)
    acol = A("acol", [128, 2, 16], F32)
    dcol = A("dcol_sb", [128, 2], F32)
    hcar = A("hcar", [128, 2, 16], F32)

    s.dma("sync", "sy_c", dtw[:], dtwd)
    s.dma("sync", "sy_c", dtb[:], dtbd)
    s.dma("sync", "sy_c", alog[:], alogd.rearrange("p (g s) -> p g s", s=16))
    t_c = s.dma("sync", "sy_c", dcol[:], dcold)
    s.wait("vector", "sy_c", t_c)
    s.op("vector", lambda E: E.tensor_scalar(acol[:], alog[:], -6.0, 6.0, ALU.max, ALU.min), "dve_ac")
    s.wait("scalar", "dve_ac", s.cnt["dve_ac"])
    s.op("scalar", lambda E: E.activation(acol[:], acol[:], AF.Exp), "act_ac")
    s.wait("vector", "act_ac", s.cnt["act_ac"])
    s.op("vector", lambda E: E.tensor_scalar(acol[:], acol[:], -1.0, None, ALU.mult), "dve_ac2")
    s.wait("scalar", "dve_ac2", s.cnt["dve_ac2"])

    nit = 0
    for dg in range(2):
        for sgi in range(NSEGS):
            seg0 = sgi * SEG
            first = (sgi % (NSEGS // B) == 0)
            sl = slice(seg0, seg0 + SEG)
            if nit >= 1:
                s.wait("gpsimd", "dve_yo", nit)
                s.wait("gpsimd", "dve_dtp", nit)
            t_l = s.dma("gpsimd", "gx" + str(nit % 2), xcs[:], xcd.rearrange("(g p) t -> g p t", p=128)[dg][:, sl])
            s.dma("gpsimd", "gb" + str(nit % 2), dtraw_b[:], bcdtd[0:1, sl].broadcast_to((128, SEG)))
            s.wait("vector", "gb" + str(nit % 2), s.cnt["gb" + str(nit % 2)])
            if nit >= 1:
                s.wait("vector", "act_sp", s.cnt["act_sp"])
                s.wait("vector", "act_a", s.cnt["act_a"])
            s.op("vector", lambda E, dg=dg: E.tensor_scalar(dtp[:], dtraw_b[:], dtw[:, dg:dg + 1],
                                                            dtb[:, dg:dg + 1], ALU.mult, ALU.add), "dve_dtp")
            s.wait("scalar", "dve_dtp", s.cnt["dve_dtp"])
            s.op("scalar", lambda E: E.activation(sp1[:], dtp[:], AF.Abs), "act_sp")
            s.op("scalar", lambda E: E.activation(sp1[:], sp1[:], AF.Exp, scale=-1.0), "act_sp")
            s.op("scalar", lambda E: E.activation(sp1[:], sp1[:], AF.Ln, bias=1.0), "act_sp")
            s.wait("vector", "act_sp", s.cnt["act_sp"])
            s.op("vector", lambda E: E.tensor_scalar(dts[:], dtp[:], 0.0, None, ALU.max))
            s.op("vector", lambda E: E.tensor_tensor(dts[:], dts[:], sp1[:], ALU.add), "dve_dts")
            s.wait("vector", "gx" + str(nit % 2), t_l)
            s.op("vector", lambda E: E.tensor_tensor(ut[:], dts[:], xcs[:], ALU.mult))
            s.op("vector", lambda E: E.memset(yt[:], 0.0), "dve_u")
            for si in range(16):
                gsi = nit * 16 + si
                if gsi >= 1:
                    s.wait("scalar", "dve_scan", gsi)
                if si == 0:
                    s.wait("scalar", "dve_dts", s.cnt["dve_dts"])
                s.op("scalar",
                     lambda E, dg=dg, si=si: E.activation(at[:], dts[:], AF.Exp,
                                                          scale=acol[:, dg, si:si + 1]),
                     "act_a")
                bb = (bbcA, bbcB)[si % 2]
                cb_ = (cbcA, cbcB)[si % 2]
                if gsi >= 2:
                    s.wait("gpsimd", "dve_bb", gsi - 1)
                s.dma("gpsimd", "gs" + str(si % 2), bb[:], bcdtd[1 + si:2 + si, sl].broadcast_to((128, SEG)))
                s.wait("vector", "gs" + str(si % 2), s.cnt["gs" + str(si % 2)])
                s.op("vector", lambda E, bb=bb: E.tensor_tensor(bt[:], ut[:], bb[:], ALU.mult), "dve_bb")
                s.wait("vector", "act_a", s.cnt["act_a"])
                if first:
                    s.op("vector", lambda E: E.tensor_tensor_scan(ht[:], at[:], bt[:], 0.0,
                                                                  ALU.mult, ALU.add), "dve_scan")
                else:
                    s.op("vector", lambda E, dg=dg, si=si: E.tensor_tensor_scan(
                        ht[:], at[:], bt[:], hcar[:, dg, si:si + 1], ALU.mult, ALU.add), "dve_scan")
                s.op("vector", lambda E, dg=dg, si=si: E.tensor_scalar(
                    hcar[:, dg, si:si + 1], ht[:, SEG - 1:SEG], -10000.0, 10000.0, ALU.max, ALU.min))
                if gsi >= 2:
                    s.wait("gpsimd", "dve_cc", gsi - 1)
                s.dma("gpsimd", "gc" + str(si % 2), cb_[:], bcdtd[17 + si:18 + si, sl].broadcast_to((128, SEG)))
                s.wait("vector", "gc" + str(si % 2), s.cnt["gc" + str(si % 2)])
                s.op("vector", lambda E, cb_=cb_: E.tensor_tensor(tmp[:], ht[:], cb_[:], ALU.mult), "dve_cc")
                s.op("vector", lambda E: E.tensor_tensor(yt[:], yt[:], tmp[:], ALU.add), "dve_y")
            if nit >= 1:
                s.wait("vector", "gq_y", nit * 16)
            s.op("vector", lambda E, dg=dg: E.scalar_tensor_tensor(
                yo[:], xcs[:], dcol[:, dg:dg + 1], yt[:], ALU.mult, ALU.add), "dve_yo")
            s.wait("gpsimd", "dve_yo", s.cnt["dve_yo"])
            s.dma("gpsimd", "gq_y", y_out.rearrange("(g p) t -> g p t", p=128)[dg][:, sl], yo[:])
            nit += 1
    s.wait("gpsimd", "gq_y", s.cnt["gq_y"])
    return s.build()


def build_l1a_post():
    s = Sched("l1a_post")
    nc = s.nc
    xT3 = nc.dram_tensor("xT3", [DM, TLOC], F32, kind="ExternalInput").ap()
    yd = nc.dram_tensor("yd", [2048, TLOC], BF16, kind="ExternalInput").ap()
    zd = nc.dram_tensor("zd", [2048, TLOC], BF16, kind="ExternalInput").ap()
    wod = nc.dram_tensor("wod", [2048, DM], F32, kind="ExternalInput").ap()
    gw1d = nc.dram_tensor("gw1d", [DM, DM], F32, kind="ExternalInput").ap()
    gb1d = nc.dram_tensor("gb1d", [128, 8], F32, kind="ExternalInput").ap()
    gw2d = nc.dram_tensor("gw2d", [DM, 8], F32, kind="ExternalInput").ap()
    onesd = nc.dram_tensor("ones", [128, 1], F32, kind="ExternalInput").ap()
    x3_out = nc.dram_tensor("x3_out", [DM, TLOC], F32, kind="ExternalOutput").ap()
    xn3_out = nc.dram_tensor("xn3_out", [128, 8 * TLOC], BF16, kind="ExternalOutput").ap()
    gp_out = nc.dram_tensor("gp_out", [1, 8], F32, kind="ExternalOutput").ap()
    rows_scr = nc.dram_tensor("rows_scr", [1, 2 * TLOC], F32, kind="Internal").ap()
    rowsb_scr = nc.dram_tensor("rowsb_scr", [1, 2 * TLOC], BF16, kind="Internal").ap()

    A = nc.alloc_sbuf_tensor
    ygA = A("ygA", [128, 16, 512], BF16)
    ygB = ygA
    zt = A("zt", [128, 16, 512], BF16)
    wout = A("wout", [128, 8, TLOC], BF16)
    x3 = A("x3", [128, 8, TLOC], F32)
    xn3 = A("xn3", [128, 8, TLOC], BF16)
    wobA = A("wobA", [128, 16, 128], BF16)
    wobB = A("wobB", [128, 16, 128], BF16)
    gwA = A("gwA", [128, 8, 128], BF16)
    gwB = A("gwB", [128, 8, 128], BF16)
    gw2b = A("gw2b", [128, 8, 8], BF16)
    gb1c = A("gb1c", [128, 8], F32)
    h1t = A("h1t", [128, 8, 512], BF16)
    wsqA = A("wsqA", [128, 512], F32)
    wsqB = A("wsqB", [128, 512], F32)
    onescol = A("onescol", [128, 1], F32)
    onescolb = A("onescolb", [128, 1], BF16)
    gacc = A("gacc", [8, 4], F32)
    gout = A("gout", [1, 8], F32)
    stat = A("stat", [128, 32], F32)
    sw = {k: A(k, [128, 16], F32) for k in ("stm", "stv", "str_", "stt1", "stt2")}
    sw["mrow"] = A("mrow", [128, 16], BF16)
    sw["rrow"] = A("rrow", [128, 16], BF16)
    srow2 = A("srow2", [1, 1024], F32)
    mb = A("mb", [128, TLOC], BF16)
    rb = A("rb", [128, TLOC], BF16)
    xout = A("xout", [128, 512], F32)
    xresA = A("xresA", [128, 512], F32)
    xresB = A("xresB", [128, 512], F32)

    P = nc.alloc_psum_tensor
    ps_proj = P("ps_proj", [128, 512], F32)
    ps_sx = P("ps_sx", [1, 512], F32)
    ps_sq = P("ps_sq", [1, 512], F32)
    ps_g = P("ps_g", [8, 512], F32)

    s.dma("sync", "sy_x", x3[:], xT3.rearrange("(c p) t -> p c t", p=128))
    s.dma("sync", "sy_c", onescol[:], onesd)
    s.dma("sync", "sy_c", gb1c[:], gb1d)
    s.dma("gpsimd", "gq_b", onescolb[:], onesd)
    s.dma("gpsimd", "gq_b", gw2b[:], gw2d.rearrange("(c p) o -> p c o", p=128))
    wo_r = wod.rearrange("(c p) (o q) -> o p c q", p=128, q=128)
    gw1_r = gw1d.rearrange("(c p) (o q) -> o p c q", p=128, q=128)

    s.wait("tensor", "gq_b", s.cnt["gq_b"])
    for tt in range(4):
        yg = (ygA, ygB)[tt % 2]
        sl = slice(tt * 512, (tt + 1) * 512)
        gy = "gy" + str(tt % 2)
        if tt >= 1:
            s.wait("gpsimd", "pe_wo", tt * 8)
        s.dma("gpsimd", gy, yg[:], yd[:, sl].rearrange("(g p) t -> p g t", p=128))
        t_z = s.dma("gpsimd", gy, zt[:], zd[:, sl].rearrange("(g p) t -> p g t", p=128))
        s.wait("vector", gy, t_z)
        if tt >= 1:
            s.wait("vector", "pe_wo", tt * 8)
        for g in range(16):
            s.op("vector",
                 lambda E, yg=yg, g=g: E.tensor_tensor(yg[:, g, :], yg[:, g, :], zt[:, g, :], ALU.mult),
                 "dve_yg" if g == 15 else None)
        s.wait("tensor", "dve_yg", s.cnt["dve_yg"])
        for oc in range(8):
            j = tt * 8 + oc
            wob = (wobA, wobB)[j % 2]
            wv = "gw_o" + str(j % 2)
            if j >= 2:
                s.wait("gpsimd", "pe_wo", j - 1)
            s.dma("gpsimd", wv, wob[:], wo_r[oc])
            if j >= 1:
                s.wait("tensor", "evac_w", j)
            s.wait("tensor", wv, s.cnt[wv])
            for g in range(16):
                s.op("tensor",
                     lambda E, yg=yg, g=g, oc=oc, wob=wob: E.matmul(
                         ps_proj[:], wob[:, g, :], yg[:, g, :],
                         start=(g == 0), stop=(g == 15)),
                     "pe_wo" if g == 15 else None)
            s.wait("vector", "pe_wo", s.cnt["pe_wo"])
            s.op("vector",
                 lambda E, oc=oc, sl=sl: E.tensor_copy(wout[:, oc, sl], ps_proj[:]), "evac_w")
    t_wout = s.cnt["evac_w"]

    s.wait("scalar", "evac_w", t_wout)
    s.wait("tensor", "evac_w", t_wout)
    s.wait("tensor", "gq_b", s.cnt["gq_b"])
    s.wait("tensor", "sy_c", s.cnt["sy_c"])
    for tt in range(4):
        for oc in range(8):
            j = tt * 8 + oc
            ws = (wsqA, wsqB)[j % 2]
            if j >= 2:
                s.wait("scalar", "pe_wsq", j - 1)
            s.op("scalar",
                 lambda E, ws=ws, oc=oc, tt=tt: E.activation(
                     ws[:], wout[:, oc, tt * 512:(tt + 1) * 512], AF.Square),
                 "act_wsq")
            if oc == 0 and tt >= 1:
                s.wait("tensor", "gq_wst", tt * 32)
            s.op("tensor",
                 lambda E, oc=oc, tt=tt: E.matmul(ps_sx[:], onescolb[:],
                                                  wout[:, oc, tt * 512:(tt + 1) * 512],
                                                  start=(oc == 0), stop=(oc == 7)),
                 "pe_wst" if oc == 7 else None)
            s.wait("tensor", "act_wsq", s.cnt["act_wsq"])
            s.op("tensor",
                 lambda E, ws=ws, oc=oc: E.matmul(ps_sq[:], onescol[:], ws[:],
                                                  start=(oc == 0), stop=(oc == 7)),
                 "pe_wsq")
        s.wait("vector", "pe_wst", s.cnt["pe_wst"])
        s.wait("vector", "pe_wsq", s.cnt["pe_wsq"])
        if tt >= 1:
            s.wait("vector", "gq_wst", tt * 32)
        s.op("vector", lambda E: E.tensor_copy(srow2[0:1, 0:512], ps_sx[:]))
        s.op("vector", lambda E: E.tensor_copy(srow2[0:1, 512:1024], ps_sq[:]), "dve_wsr")
        s.wait("gpsimd", "dve_wsr", s.cnt["dve_wsr"])
        s.dma("gpsimd", "gq_wst", rows_scr[0:1, tt * 512:(tt + 1) * 512], srow2[0:1, 0:512])
        s.dma("gpsimd", "gq_wst", rows_scr[0:1, TLOC + tt * 512:TLOC + (tt + 1) * 512], srow2[0:1, 512:1024])
    t_rows = emit_stats_rows_to_sbuf(s, stat, rows_scr, wait_q="gq_wst", wait_v=s.cnt["gq_wst"])
    s.wait("vector", "gq", t_rows)
    t_bc2 = emit_stats_finish(s, "out", stat, sw, mb, rb, rows_scr, rowsb_scr, DM)
    s.wait("vector", "gq", t_bc2)
    s.wait("vector", "sy_x", s.cnt["sy_x"])
    x3r = x3_out.rearrange("(c p) (u t) -> c u p t", p=128, t=512)
    for c in range(8):
        for u in range(4):
            cu = c * 4 + u
            sl = slice(u * 512, (u + 1) * 512)
            s.op("vector", lambda E, c=c, sl=sl: E.tensor_tensor(xout[:], wout[:, c, sl], mb[:, sl], ALU.subtract))
            s.op("vector", lambda E, sl=sl: E.tensor_tensor(xout[:], xout[:], rb[:, sl], ALU.mult))
            s.op("vector", lambda E, c=c, sl=sl: E.tensor_tensor(x3[:, c, sl], xout[:], x3[:, c, sl], ALU.add),
                 "dve_out")
            s.wait("gpsimd", "dve_out", s.cnt["dve_out"])
            s.dma("gpsimd", "gq_out", x3r[c, u], x3[:, c, sl])
    t_x3 = s.cnt["dve_out"]

    s.wait("scalar", "dve_out", t_x3)
    s.wait("tensor", "dve_out", t_x3)
    s.wait("scalar", "sy_x", s.cnt["sy_x"])
    emit_ln_stats(s, "g", x3, 8, onescol, (wsqA, wsqB), ps_sx, ps_sq, rows_scr, srow2)
    t_rows2 = emit_stats_rows_to_sbuf(s, stat, rows_scr, wait_q="gq_stg", wait_v=s.cnt["gq_stg"])
    s.wait("vector", "gq", t_rows2)
    t_bc3 = emit_stats_finish(s, "g2", stat, sw, mb, rb, rows_scr, rowsb_scr, DM)
    s.wait("vector", "gq", t_bc3)
    for c in range(8):
        for u in range(4):
            sl = slice(u * 512, (u + 1) * 512)
            s.op("vector", lambda E, c=c, sl=sl: E.tensor_tensor(wsqA[:], x3[:, c, sl], mb[:, sl], ALU.subtract))
            s.op("vector", lambda E, c=c, sl=sl: E.tensor_tensor(xn3[:, c, sl], wsqA[:], rb[:, sl], ALU.mult),
                 "dve_xn3" if (c == 7 and u == 3) else None)
    t_xn3 = s.cnt["dve_xn3"]
    s.wait("gpsimd", "dve_xn3", t_xn3)
    s.dma("gpsimd", "gq", xn3_out, xn3[:].rearrange("p c t -> p (c t)"))
    s.wait("tensor", "dve_xn3", t_xn3)
    s.op("vector", lambda E: E.memset(gacc[:], 0.0), "dve_gz")
    for tt in range(4):
        sl = slice(tt * 512, (tt + 1) * 512)
        for oc in range(8):
            j = tt * 8 + oc
            gwt = (gwA, gwB)[j % 2]
            gv = "gw_g" + str(j % 2)
            if j >= 2:
                s.wait("gpsimd", "pe_h1", j - 1)
            s.dma("gpsimd", gv, gwt[:], gw1_r[oc])
            if j >= 1:
                s.wait("tensor", "evac_h", j)
            s.wait("tensor", gv, s.cnt[gv])
            for c in range(8):
                s.op("tensor",
                     lambda E, c=c, oc=oc, sl=sl, gwt=gwt: E.matmul(
                         ps_proj[:], gwt[:, c, :], xn3[:, c, sl],
                         start=(c == 0), stop=(c == 7)),
                     "pe_h1" if c == 7 else None)
            s.wait("vector", "pe_h1", s.cnt["pe_h1"])
            s.op("vector",
                 lambda E, oc=oc: E.tensor_scalar(h1t[:, oc, :], ps_proj[:], gb1c[:, oc:oc + 1],
                                                  0.0, ALU.add, ALU.max),
                 "evac_h")
        s.wait("tensor", "evac_h", s.cnt["evac_h"])
        for c in range(8):
            s.op("tensor",
                 lambda E, c=c: E.matmul(ps_g[:], gw2b[:, c, :], h1t[:, c, :],
                                         start=(c == 0), stop=(c == 7)),
                 "pe_g" if c == 7 else None)
        s.wait("vector", "pe_g", s.cnt["pe_g"])
        s.op("vector", lambda E, tt=tt: E.tensor_reduce(gacc[:, tt:tt + 1], ps_g[:], AX.X, ALU.add), "dve_g")
    s.op("vector", lambda E: E.tensor_reduce(gout[0:1, 0:8] if False else stat[0:8, 0:1],
                                             gacc[0:8, 0:4], AX.X, ALU.add), "dve_g2")
    s.wait("gpsimd", "dve_g2", s.cnt["dve_g2"])
    s.dma("gpsimd", "gq", gp_out, stat[0:8, 0:1])
    s.wait("gpsimd", "gq", s.cnt["gq"])
    s.wait("gpsimd", "gq_out", s.cnt["gq_out"])
    s.wait("gpsimd", "gq_wst", s.cnt["gq_wst"])
    return s.build()


def build_l1f_mw():
    s = Sched("l1f_mw")
    nc = s.nc
    FQ = 1024
    gpartsd = nc.dram_tensor("gparts", [1, NSEG * 8], F32, kind="ExternalInput").ap()
    gb2d = nc.dram_tensor("gb2", [1, 8], F32, kind="ExternalInput").ap()
    w1Tq = nc.dram_tensor("w1Tq", [DM, FQ], F32, kind="ExternalInput").ap()
    tw1Tq = nc.dram_tensor("tw1Tq", [8 * DM, FQ], F32, kind="ExternalInput").ap()
    w2Tq = nc.dram_tensor("w2Tq", [FQ, DM], F32, kind="ExternalInput").ap()
    tw2Tq = nc.dram_tensor("tw2Tq", [8 * FQ, DM], F32, kind="ExternalInput").ap()
    b1cd = nc.dram_tensor("b1c", [128, 32], F32, kind="ExternalInput").ap()
    tb1cd = nc.dram_tensor("tb1c", [128, 8 * 32], F32, kind="ExternalInput").ap()
    b2cd = nc.dram_tensor("b2c", [128, 8], F32, kind="ExternalInput").ap()
    tb2cd = nc.dram_tensor("tb2c", [128, 8 * 8], F32, kind="ExternalInput").ap()
    mw1_out = nc.dram_tensor("mw1_out", [DM, FQ], BF16, kind="ExternalOutput").ap()
    mw2_out = nc.dram_tensor("mw2_out", [FQ, DM], BF16, kind="ExternalOutput").ap()
    mb1_out = nc.dram_tensor("mb1_out", [128, 32], F32, kind="ExternalOutput").ap()
    mb2_out = nc.dram_tensor("mb2_out", [128, 8], F32, kind="ExternalOutput").ap()
    g_scr = nc.dram_tensor("g_scr", [1, 8], F32, kind="Internal").ap()

    A = nc.alloc_sbuf_tensor
    gparts = A("gparts_sb", [1, NSEG * 8], F32)
    gb2 = A("gb2_sb", [1, 8], F32)
    grow = A("grow", [1, 8], F32)
    gcols = A("gcols", [128, 8], F32)
    acc = A("acc", [128, 512], F32)
    accb = A("accb", [128, 512], BF16)
    twA = A("twA", [128, 512], F32)
    twB = A("twB", [128, 512], F32)
    b1c = A("b1c_sb", [128, 32], F32)
    tb1c = A("tb1c_sb", [128, 8, 32], F32)
    b2c = A("b2c_sb", [128, 8], F32)
    tb2c = A("tb2c_sb", [128, 8, 8], F32)
    macc = A("macc", [128, 32], F32)

    s.dma("sync", "sy_c", gparts[:], gpartsd)
    s.dma("sync", "sy_c", gb2[:], gb2d)
    s.dma("sync", "sy_c", b1c[:], b1cd)
    s.dma("sync", "sy_c", tb1c[:], tb1cd.rearrange("p (e f) -> p e f", e=8))
    s.dma("sync", "sy_c", b2c[:], b2cd)
    t_c = s.dma("sync", "sy_c", tb2c[:], tb2cd.rearrange("p (e f) -> p e f", e=8))
    s.wait("vector", "sy_c", t_c)
    s.op("vector", lambda E: E.tensor_reduce(grow[:], gparts[:].rearrange("a (j e) -> a e j", e=8), AX.X, ALU.add))
    s.op("vector", lambda E: E.tensor_scalar(grow[:], grow[:], 1.0 / T, None, ALU.mult))
    s.op("vector", lambda E: E.tensor_tensor(grow[:], grow[:], gb2[:], ALU.add), "dve_g")
    s.wait("gpsimd", "dve_g", s.cnt["dve_g"])
    s.dma("gpsimd", "gq", g_scr, grow[:])
    s.wait("gpsimd", "gq", s.cnt["gq"])
    t_g = s.dma("gpsimd", "gq", gcols[:], g_scr.broadcast_to((128, 8)))
    s.wait("vector", "gq", t_g)

    s.op("vector", lambda E: E.tensor_copy(macc[:], b1c[:]))
    for e in range(8):
        s.op("vector",
             lambda E, e=e: E.scalar_tensor_tensor(macc[:], tb1c[:, e, :], gcols[:, e:e + 1], macc[:],
                                                   ALU.mult, ALU.add),
             "dve_mb1" if e == 7 else None)
    s.wait("gpsimd", "dve_mb1", s.cnt["dve_mb1"])
    s.dma("gpsimd", "gq", mb1_out, macc[:])
    s.wait("vector", "gq", s.cnt["gq"])
    s.op("vector", lambda E: E.tensor_copy(macc[:, 0:8], b2c[:]))
    for e in range(8):
        s.op("vector",
             lambda E, e=e: E.scalar_tensor_tensor(macc[:, 0:8], tb2c[:, e, :], gcols[:, e:e + 1],
                                                   macc[:, 0:8], ALU.mult, ALU.add),
             "dve_mb2" if e == 7 else None)
    s.wait("gpsimd", "dve_mb2", s.cnt["dve_mb2"])
    s.dma("gpsimd", "gq", mb2_out, macc[:, 0:8])

    def mw_stream(baseT, twT, out_d, tag, prev_tag=None):
        base_r = baseT.rearrange("(c p) (u o) -> c u p o", p=128, o=512)
        tw_r = twT.rearrange("(e c p) (u o) -> e c u p o", p=128, o=512, c=8)
        out_r = out_d.rearrange("(c p) (u o) -> c u p o", p=128, o=512)
        for k in range(16):
            c, u = k // 2, k % 2
            wq0 = "ld0" + tag
            if k == 0 and prev_tag is not None:
                s.wait("sync", "dve_acc" + prev_tag, s.cnt["dve_acc" + prev_tag])
            if k >= 1:
                s.wait("sync", "dve_acc" + tag, k * 9)
            s.dma("sync", wq0, acc[:], base_r[c, u])
            s.wait("vector", wq0, s.cnt[wq0])
            for e in range(8):
                wt = (twA, twB)[e % 2]
                wq = "ld" + tag + str(e % 2)
                if e >= 2:
                    s.wait("sync", "dve_acc" + tag, k * 9 + e - 1)
                s.dma("sync", wq, wt[:], tw_r[e, c, u])
                s.wait("vector", wq, s.cnt[wq])
                s.op("vector",
                     lambda E, wt=wt, e=e: E.scalar_tensor_tensor(acc[:], wt[:], gcols[:, e:e + 1],
                                                                  acc[:], ALU.mult, ALU.add),
                     "dve_acc" + tag)
            s.op("vector", lambda E: E.tensor_copy(accb[:], acc[:]), "dve_acc" + tag)
            s.wait("gpsimd", "dve_acc" + tag, s.cnt["dve_acc" + tag])
            if k >= 1:
                s.wait("gpsimd", "gq_w" + tag, (k - 1) * 16)
            s.dma("gpsimd", "gq_w" + tag, out_r[c, u], accb[:])
            s.wait("vector", "gq_w" + tag, s.cnt["gq_w" + tag])

    mw_stream(w1Tq, tw1Tq, mw1_out, "a")
    mw_stream(w2Tq, tw2Tq, mw2_out, "b", prev_tag="a")
    s.wait("gpsimd", "gq", s.cnt["gq"])
    s.wait("gpsimd", "gq_wa", s.cnt["gq_wa"])
    s.wait("gpsimd", "gq_wb", s.cnt["gq_wb"])
    return s.build()


def build_l1f_main():
    s = Sched("l1f_main")
    nc = s.nc
    xT4 = nc.dram_tensor("xT4", [DM, TLOC], F32, kind="ExternalInput").ap()
    xn3d = nc.dram_tensor("xn3d", [128, 8 * TLOC], BF16, kind="ExternalInput").ap()
    mw1d = nc.dram_tensor("mw1d", [DM, 4096], BF16, kind="ExternalInput").ap()
    mw2d = nc.dram_tensor("mw2d", [4096, DM], BF16, kind="ExternalInput").ap()
    mb1d = nc.dram_tensor("mb1d", [128, 32], F32, kind="ExternalInput").ap()
    mb2d = nc.dram_tensor("mb2d", [128, 8], F32, kind="ExternalInput").ap()
    onesd = nc.dram_tensor("ones", [128, 1], F32, kind="ExternalInput").ap()
    x4_out = nc.dram_tensor("x4_out", [DM, TLOC], F32, kind="ExternalOutput").ap()
    rows_scr = nc.dram_tensor("rows_scr", [1, 2 * TLOC], F32, kind="Internal").ap()
    rowsb_scr = nc.dram_tensor("rowsb_scr", [1, 2 * TLOC], BF16, kind="Internal").ap()

    A = nc.alloc_sbuf_tensor
    xn3 = A("xn3", [128, 8, TLOC], BF16)
    ub = A("ub", [128, 32, 512], BF16)
    wout = A("wout", [128, 8, TLOC], BF16)
    x4 = A("x4", [128, 8, TLOC], F32)
    mwA = A("mwA", [128, 8, 128], BF16)
    mwB = A("mwB", [128, 8, 128], BF16)
    mw2A = A("mw2A", [128, 32, 128], BF16)
    mw2B = A("mw2B", [128, 32, 128], BF16)
    mb1c = A("mb1c", [128, 32], F32)
    mb2c = A("mb2c", [128, 8], F32)
    onescol = A("onescol", [128, 1], F32)
    onescolb = A("onescolb", [128, 1], BF16)
    wsqA = A("wsqA", [128, 512], F32)
    wsqB = A("wsqB", [128, 512], F32)
    stat = A("stat", [128, 32], F32)
    sw = {k: A(k, [128, 16], F32) for k in ("stm", "stv", "str_", "stt1", "stt2")}
    sw["mrow"] = A("mrow", [128, 16], BF16)
    sw["rrow"] = A("rrow", [128, 16], BF16)
    srow2 = A("srow2", [1, 1024], F32)
    mb = A("mb", [128, TLOC], BF16)
    rb = A("rb", [128, TLOC], BF16)
    xout = A("xout", [128, 512], F32)

    P = nc.alloc_psum_tensor
    ps_proj = P("ps_proj", [128, 512], F32)
    ps_sx = P("ps_sx", [1, 512], F32)
    ps_sq = P("ps_sq", [1, 512], F32)

    s.dma("sync", "sy_x", x4[:], xT4.rearrange("(c p) t -> p c t", p=128))
    s.dma("sync", "sy_x", xn3[:].rearrange("p c t -> p (c t)"), xn3d)
    s.dma("sync", "sy_c", mb1c[:], mb1d)
    s.dma("sync", "sy_c", mb2c[:], mb2d)
    s.dma("sync", "sy_c", onescol[:], onesd)
    s.dma("gpsimd", "gq_b", onescolb[:], onesd)
    mw1_r = mw1d.rearrange("(c p) (f q) -> f c p q", p=128, q=128)
    mw2_r = mw2d.rearrange("(f p) (o q) -> o f p q", p=128, q=128)

    s.wait("tensor", "sy_x", s.cnt["sy_x"])
    s.wait("tensor", "sy_c", s.cnt["sy_c"])
    s.wait("tensor", "gq_b", s.cnt["gq_b"])
    for tt in range(4):
        sl = slice(tt * 512, (tt + 1) * 512)
        if tt >= 1:
            s.wait("tensor", "evac_w", tt * 8)
        for fc in range(32):
            j = tt * 32 + fc
            mw = (mwA, mwB)[j % 2]
            wv = "mw" + str(j % 2)
            if j >= 2:
                s.wait("gpsimd", "pe_u", j - 1)
            for c in range(8):
                s.dma("gpsimd", wv, mw[:, c, :], mw1_r[fc, c])
            if j >= 1:
                s.wait("tensor", "act_u", j)
            s.wait("tensor", wv, s.cnt[wv])
            for c in range(8):
                s.op("tensor",
                     lambda E, mw=mw, c=c, sl=sl: E.matmul(
                         ps_proj[:], mw[:, c, :], xn3[:, c, sl],
                         start=(c == 0), stop=(c == 7)),
                     "pe_u" if c == 7 else None)
            s.wait("scalar", "pe_u", s.cnt["pe_u"])
            s.op("scalar",
                 lambda E, fc=fc: E.activation(ub[:, fc, :], ps_proj[:], AF.Gelu_apprx_tanh,
                                               bias=mb1c[:, fc:fc + 1]),
                 "act_u")
        s.wait("tensor", "act_u", s.cnt["act_u"])
        for oc in range(8):
            j2 = tt * 8 + oc
            mw2 = (mw2A, mw2B)[j2 % 2]
            wv2 = "mx" + str(j2 % 2)
            if j2 >= 2:
                s.wait("gpsimd", "pe_w2", j2 - 1)
            for fc in range(32):
                s.dma("gpsimd", wv2, mw2[:, fc, :], mw2_r[oc, fc])
            if j2 >= 1:
                s.wait("tensor", "evac_w", j2)
            s.wait("tensor", wv2, s.cnt[wv2])
            for fc in range(32):
                s.op("tensor",
                     lambda E, mw2=mw2, fc=fc: E.matmul(
                         ps_proj[:], mw2[:, fc, :], ub[:, fc, :],
                         start=(fc == 0), stop=(fc == 31)),
                     "pe_w2" if fc == 31 else None)
            s.wait("vector", "pe_w2", s.cnt["pe_w2"])
            s.op("vector",
                 lambda E, oc=oc, sl=sl: E.tensor_scalar(wout[:, oc, sl], ps_proj[:],
                                                         mb2c[:, oc:oc + 1], None, ALU.add),
                 "evac_w")
    t_wout = s.cnt["evac_w"]

    s.wait("scalar", "evac_w", t_wout)
    s.wait("tensor", "evac_w", t_wout)
    for tt in range(4):
        for oc in range(8):
            j = tt * 8 + oc
            ws = (wsqA, wsqB)[j % 2]
            if j >= 2:
                s.wait("scalar", "pe_wsq", j - 1)
            s.op("scalar",
                 lambda E, ws=ws, oc=oc, tt=tt: E.activation(
                     ws[:], wout[:, oc, tt * 512:(tt + 1) * 512], AF.Square),
                 "act_wsq")
            if oc == 0 and tt >= 1:
                s.wait("tensor", "gq_wst", tt * 32)
            s.op("tensor",
                 lambda E, oc=oc, tt=tt: E.matmul(ps_sx[:], onescolb[:],
                                                  wout[:, oc, tt * 512:(tt + 1) * 512],
                                                  start=(oc == 0), stop=(oc == 7)),
                 "pe_wst" if oc == 7 else None)
            s.wait("tensor", "act_wsq", s.cnt["act_wsq"])
            s.op("tensor",
                 lambda E, ws=ws, oc=oc: E.matmul(ps_sq[:], onescol[:], ws[:],
                                                  start=(oc == 0), stop=(oc == 7)),
                 "pe_wsq")
        s.wait("vector", "pe_wst", s.cnt["pe_wst"])
        s.wait("vector", "pe_wsq", s.cnt["pe_wsq"])
        if tt >= 1:
            s.wait("vector", "gq_wst", tt * 32)
        s.op("vector", lambda E: E.tensor_copy(srow2[0:1, 0:512], ps_sx[:]))
        s.op("vector", lambda E: E.tensor_copy(srow2[0:1, 512:1024], ps_sq[:]), "dve_wsr")
        s.wait("gpsimd", "dve_wsr", s.cnt["dve_wsr"])
        s.dma("gpsimd", "gq_wst", rows_scr[0:1, tt * 512:(tt + 1) * 512], srow2[0:1, 0:512])
        s.dma("gpsimd", "gq_wst", rows_scr[0:1, TLOC + tt * 512:TLOC + (tt + 1) * 512], srow2[0:1, 512:1024])
    t_rows = emit_stats_rows_to_sbuf(s, stat, rows_scr, wait_q="gq_wst", wait_v=s.cnt["gq_wst"])
    s.wait("vector", "gq", t_rows)
    t_bc2 = emit_stats_finish(s, "out", stat, sw, mb, rb, rows_scr, rowsb_scr, DM)
    s.wait("vector", "gq", t_bc2)
    x4r = x4_out.rearrange("(c p) (u t) -> c u p t", p=128, t=512)
    for c in range(8):
        for u in range(4):
            cu = c * 4 + u
            sl = slice(u * 512, (u + 1) * 512)
            if cu >= 1:
                s.wait("vector", "gq_out", cu * 16)
            s.op("vector", lambda E, c=c, sl=sl: E.tensor_tensor(xout[:], wout[:, c, sl], mb[:, sl], ALU.subtract))
            s.op("vector", lambda E, sl=sl: E.tensor_tensor(xout[:], xout[:], rb[:, sl], ALU.mult))
            s.op("vector", lambda E, c=c, sl=sl: E.tensor_tensor(xout[:], xout[:], x4[:, c, sl], ALU.add), "dve_out")
            s.wait("gpsimd", "dve_out", s.cnt["dve_out"])
            s.dma("gpsimd", "gq_out", x4r[c, u], xout[:])
    s.wait("gpsimd", "gq_out", s.cnt["gq_out"])
    s.wait("gpsimd", "gq", s.cnt["gq"])
    return s.build()


def _prep_common():
    import ml_dtypes
    cos2, sinx2 = _rot_tables()
    maskT, wvecT, gamvec = _decay_consts()
    identb = np.eye(128).astype(ml_dtypes.bfloat16)
    ones = np.ones((128, 1), np.float32)
    return dict(cos2=cos2, sinx2=sinx2, maskT=maskT, wvecT=wvecT,
                gamvec=gamvec, ident=identb, ones=ones)


def _core_slices():
    return [(i // NSEG, (i % NSEG) * TLOC) for i in range(NCORES)]


def run_l0a(inputs):
    cc = _prep_common()
    x = inputs["L0_attn_x"]
    xT_full = np.ascontiguousarray(x.transpose(0, 2, 1))
    wkvT = np.ascontiguousarray(np.hstack([inputs["L0_attn_Wk"].T, inputs["L0_attn_Wv"].T]))
    wqgT = np.ascontiguousarray(np.hstack([inputs["L0_attn_Wq"].T, inputs["L0_attn_Wg"].T]))
    woT = np.ascontiguousarray(inputs["L0_attn_Wo"].T)
    slices = _core_slices()

    nc1 = _BUILD_CACHE.setdefault("l0a1", build_l0a1())
    in_maps = []
    for b, t0 in slices:
        in_maps.append({
            "xT": np.ascontiguousarray(xT_full[b, :, t0:t0 + TLOC]),
            "wkvT": wkvT,
            "cos2": np.ascontiguousarray(cc["cos2"][:, t0:t0 + TLOC]),
            "sinx2": np.ascontiguousarray(cc["sinx2"][:, t0:t0 + TLOC]),
            "wvecT": cc["wvecT"],
            "g128cols": _g128cols(),
            "identd": cc["ident"],
            "ones": cc["ones"],
        })
    res1 = run_bass_kernel_spmd(nc1, in_maps, core_ids=list(range(NCORES)))
    r1 = res1.results

    nc2 = _BUILD_CACHE.setdefault("l0a2", build_l0a2())
    in_maps2 = []
    for i, (b, t0) in enumerate(slices):
        seg = i % NSEG
        batch_cores = [b * NSEG + j for j in range(NSEG)]
        summs = np.concatenate([r1[j]["summ_out"] for j in batch_cores], axis=0)
        in_maps2.append({
            "xT": np.ascontiguousarray(xT_full[b, :, t0:t0 + TLOC]),
            "xn_in": r1[i]["xn_out"],
            "krot_in": r1[i]["krot_out"],
            "vT_in": r1[i]["vT_out"],
            "st_in": r1[i]["st_out"],
            "summs": summs,
            "wqgT": wqgT,
            "woT": woT,
            "cos2": np.ascontiguousarray(cc["cos2"][:, t0:t0 + TLOC]),
            "sinx2": np.ascontiguousarray(cc["sinx2"][:, t0:t0 + TLOC]),
            "maskT": np.ascontiguousarray(cc["maskT"].reshape(H * 128, 128)),
            "gamv": np.ascontiguousarray(cc["gamvec"].reshape(H * DH, 128)),
            "gpcols": _gpcols(),
            "wcomb": _wcomb(seg).reshape(NSEG * 128, 4 * 128),
            "gnw": inputs["L0_attn_gn_weight"].reshape(1, DM),
            "gnb": inputs["L0_attn_gn_bias"].reshape(1, DM),
            "identd": cc["ident"],
            "ones": cc["ones"],
        })
    res2 = run_bass_kernel_spmd(nc2, in_maps2, core_ids=list(range(NCORES)))
    x1T = np.zeros((B, DM, T), np.float32)
    for i, (b, t0) in enumerate(slices):
        x1T[b, :, t0:t0 + TLOC] = res2.results[i]["x1_out"]
    return np.ascontiguousarray(x1T.transpose(0, 2, 1))


def run_l0f(inputs, x1):
    x1T = np.ascontiguousarray(x1.transpose(0, 2, 1))
    wrkT = np.ascontiguousarray(np.hstack([inputs["L0_ffn_W_r"].T, inputs["L0_ffn_W_k"].T]))
    wvT2 = np.ascontiguousarray(inputs["L0_ffn_W_v"].T)
    mucols = np.ascontiguousarray(np.hstack([inputs["L0_ffn_mu_r"].reshape(8, 128).T,
                                             inputs["L0_ffn_mu_k"].reshape(8, 128).T]))
    ones = np.ones((128, 1), np.float32)
    nc = _BUILD_CACHE.setdefault("l0f", build_l0f())
    in_maps = []
    for i, (b, t0) in enumerate(_core_slices()):
        xe = np.zeros((DM, TLOC + 1), np.float32)
        xe[:, 1:] = x1T[b, :, t0:t0 + TLOC]
        if t0 > 0:
            xe[:, 0] = x1T[b, :, t0 - 1]
        in_maps.append({"xTe": xe, "wrkT": wrkT, "wvT2": wvT2, "mucols": mucols, "ones": ones})
    res = run_bass_kernel_spmd(nc, in_maps, core_ids=list(range(NCORES)))
    x2T = np.zeros((B, DM, T), np.float32)
    for i, (b, t0) in enumerate(_core_slices()):
        x2T[b, :, t0:t0 + TLOC] = res.results[i]["x2_out"]
    return np.ascontiguousarray(x2T.transpose(0, 2, 1))


def run_l1a(inputs, x2):
    x2T = np.ascontiguousarray(x2.transpose(0, 2, 1))
    ones = np.ones((128, 1), np.float32)
    winT = np.ascontiguousarray(inputs["L1_attn_in_proj_w"])
    xpT = np.ascontiguousarray(inputs["L1_attn_x_proj_w"])
    cw = inputs["L1_attn_conv_w"][:, 0, :]
    wconv = np.ascontiguousarray(
        cw.reshape(16, 128, 4).transpose(1, 0, 2).reshape(128, 64))
    cbcol = np.ascontiguousarray(inputs["L1_attn_conv_b"].reshape(16, 128).T)
    nc = _BUILD_CACHE.setdefault("l1a_pre", build_l1a_pre())
    in_maps = []
    for i, (b, t0) in enumerate(_core_slices()):
        xe = np.zeros((DM, TLOC + 3), np.float32)
        xe[:, 3:] = x2T[b, :, t0:t0 + TLOC]
        if t0 > 0:
            xe[:, 0:3] = x2T[b, :, t0 - 3:t0]
        in_maps.append({"xTe": xe, "winT": winT, "xpT": xpT, "wconv": wconv,
                        "cbcol": cbcol, "ones": ones})
    res = run_bass_kernel_spmd(nc, in_maps, core_ids=list(range(NCORES)))
    r = res.results
    import ml_dtypes
    xc_all = np.zeros((2048, B * T), ml_dtypes.bfloat16)
    bcdt_all = np.zeros((33, B * T), np.float32)
    for i, (b, t0) in enumerate(_core_slices()):
        g0 = b * T + t0
        xc_all[:, g0:g0 + TLOC] = r[i]["xcb_out"]
        bcdt_all[:, g0:g0 + TLOC] = r[i]["bcdt_out"]
    nc2 = _BUILD_CACHE.setdefault("l1a_scan", build_l1a_scan())
    in_maps2 = []
    dtw_full = inputs["L1_attn_dt_w"]
    dtb_full = inputs["L1_attn_dt_b"]
    alog_full = inputs["L1_attn_A_log"]
    d_full = inputs["L1_attn_D"]
    for i in range(NCORES):
        dsl = slice(i * 256, (i + 1) * 256)
        in_maps2.append({
            "xcd": np.ascontiguousarray(xc_all[dsl]),
            "bcdtd": bcdt_all,
            "dtw": np.ascontiguousarray(dtw_full[dsl].reshape(2, 128).T),
            "dtb": np.ascontiguousarray(dtb_full[dsl].reshape(2, 128).T),
            "alog": np.ascontiguousarray(
                alog_full[dsl].reshape(2, 128, 16).transpose(1, 0, 2).reshape(128, 32)),
            "dcol": np.ascontiguousarray(d_full[dsl].reshape(2, 128).T),
        })
    res2 = run_bass_kernel_spmd(nc2, in_maps2, core_ids=list(range(NCORES)))
    y_all = np.zeros((2048, B * T), ml_dtypes.bfloat16)
    for i in range(NCORES):
        y_all[i * 256:(i + 1) * 256] = res2.results[i]["y_out"]
    nc3 = _BUILD_CACHE.setdefault("l1a_post", build_l1a_post())
    wod = np.ascontiguousarray(inputs["L1_attn_out_proj_w"])
    gw1d = np.ascontiguousarray(inputs["L1_ffn_gate_w1"].T)
    gb1d = np.ascontiguousarray(inputs["L1_ffn_gate_b1"].reshape(8, 128).T)
    gw2d = np.ascontiguousarray(inputs["L1_ffn_gate_w2"].T)
    in_maps3 = []
    for i, (b, t0) in enumerate(_core_slices()):
        g0 = b * T + t0
        in_maps3.append({
            "xT3": np.ascontiguousarray(x2T[b, :, t0:t0 + TLOC]),
            "yd": np.ascontiguousarray(y_all[:, g0:g0 + TLOC]),
            "zd": r[i]["zsb_out"],
            "wod": wod, "gw1d": gw1d, "gb1d": gb1d, "gw2d": gw2d, "ones": ones,
        })
    res3 = run_bass_kernel_spmd(nc3, in_maps3, core_ids=list(range(NCORES)))
    x3T = np.zeros((B, DM, T), np.float32)
    xn3s = []
    gps = []
    for i, (b, t0) in enumerate(_core_slices()):
        x3T[b, :, t0:t0 + TLOC] = res3.results[i]["x3_out"]
        xn3s.append(res3.results[i]["xn3_out"])
        gps.append(res3.results[i]["gp_out"])
    return np.ascontiguousarray(x3T.transpose(0, 2, 1)), xn3s, gps


def run_l1f(inputs, x3, xn3s, gps):
    import ml_dtypes
    x3T = np.ascontiguousarray(x3.transpose(0, 2, 1))
    ones = np.ones((128, 1), np.float32)
    W1T = np.ascontiguousarray(inputs["L1_ffn_W1"].T)
    W2T = np.ascontiguousarray(inputs["L1_ffn_W2"].T)
    TW1T = np.ascontiguousarray(inputs["L1_ffn_TW1"].transpose(0, 2, 1))
    TW2T = np.ascontiguousarray(inputs["L1_ffn_TW2"].transpose(0, 2, 1))
    b1c = np.ascontiguousarray(inputs["L1_ffn_b1"].reshape(32, 128).T)
    tb1c = np.ascontiguousarray(
        inputs["L1_ffn_Tb1"].reshape(8, 32, 128).transpose(2, 0, 1).reshape(128, 8 * 32))
    b2c = np.ascontiguousarray(inputs["L1_ffn_b2"].reshape(8, 128).T)
    tb2c = np.ascontiguousarray(
        inputs["L1_ffn_Tb2"].reshape(8, 8, 128).transpose(2, 0, 1).reshape(128, 8 * 8))
    nc = _BUILD_CACHE.setdefault("l1f_mw", build_l1f_mw())
    in_maps = []
    for i in range(NCORES):
        bt, q = i // NSEG, i % NSEG
        gparts = np.hstack([gps[bt * NSEG + j].reshape(1, 8) for j in range(NSEG)])
        fsl = slice(q * 1024, (q + 1) * 1024)
        in_maps.append({
            "gparts": gparts.astype(np.float32),
            "gb2": inputs["L1_ffn_gate_b2"].reshape(1, 8),
            "w1Tq": np.ascontiguousarray(W1T[:, fsl]),
            "tw1Tq": np.ascontiguousarray(TW1T[:, :, fsl]).reshape(8 * DM, 1024),
            "w2Tq": np.ascontiguousarray(W2T[fsl]),
            "tw2Tq": np.ascontiguousarray(TW2T[:, fsl]).reshape(8 * 1024, DM),
            "b1c": b1c, "tb1c": tb1c, "b2c": b2c, "tb2c": tb2c,
        })
    res = run_bass_kernel_spmd(nc, in_maps, core_ids=list(range(NCORES)))
    r = res.results
    mw1 = {}
    mw2 = {}
    for bt in range(B):
        mw1[bt] = np.concatenate([r[bt * NSEG + q]["mw1_out"] for q in range(NSEG)], axis=1)
        mw2[bt] = np.concatenate([r[bt * NSEG + q]["mw2_out"] for q in range(NSEG)], axis=0)
    nc2 = _BUILD_CACHE.setdefault("l1f_main", build_l1f_main())
    in_maps2 = []
    for i, (b, t0) in enumerate(_core_slices()):
        in_maps2.append({
            "xT4": np.ascontiguousarray(x3T[b, :, t0:t0 + TLOC]),
            "xn3d": xn3s[i],
            "mw1d": np.ascontiguousarray(mw1[b]),
            "mw2d": np.ascontiguousarray(mw2[b]),
            "mb1d": r[b * NSEG]["mb1_out"],
            "mb2d": r[b * NSEG]["mb2_out"],
            "ones": ones,
        })
    res2 = run_bass_kernel_spmd(nc2, in_maps2, core_ids=list(range(NCORES)))
    x4T = np.zeros((B, DM, T), np.float32)
    for i, (b, t0) in enumerate(_core_slices()):
        x4T[b, :, t0:t0 + TLOC] = res2.results[i]["x4_out"]
    return np.ascontiguousarray(x4T.transpose(0, 2, 1))


def kernel_partial(stage, **inputs):
    x1 = run_l0a(inputs)
    if stage == "x1":
        return x1
    x2 = run_l0f(inputs, x1)
    if stage == "x2":
        return x2
    x3, xn3s, gps = run_l1a(inputs, x2)
    if stage == "x3":
        return x3
    x4 = run_l1f(inputs, x3, xn3s, gps)
    return x4


def kernel(**inputs):
    return kernel_partial("x4", **inputs)
